# version 20
# baseline (speedup 1.0000x reference)
import numpy as np
import ml_dtypes
import concourse.bass as bass
import concourse.mybir as mybir
from concourse.bass_utils import run_bass_kernel_spmd

F32 = mybir.dt.float32
BF16 = mybir.dt.bfloat16
I8 = mybir.dt.int8
AF = mybir.ActivationFunctionType
ALU = mybir.AluOpType

B, T, D = 4, 2048, 2048
H, KV, HD = 16, 4, 128
GRP = H // KV
INTER = 8192
TPC = 1024
TTS = 512
NTT = TPC // TTS
CC = D // 128
NIC = INTER // 128
EPS = 1e-5
KB = 1024

ALL_PARTS = ("ffn1", "attn", "ffn2")
EMBED_LN = True


class Sched:

    def __init__(self):
        self.q = {e: [] for e in ("tensor", "scalar", "vector", "sync", "gpsimd")}
        self.tot = {s: 0 for s in
                    ("pe", "act", "dve", "cc", "cst",
                     "w0", "w1", "w2", "w3", "w4", "w5",
                     "wd0", "wd1", "wd2", "msk", "kv0", "kv1", "rc0", "rc1",
                     "ip0", "ip1", "pcw")}

    def now(self, sem):
        return self.tot[sem]

    def op(self, eng, fname, *a, wait=(), inc=(), **k):
        wait = [w for w in wait if w is not None and w[1] > 0]
        for s, v in wait:
            assert v <= self.tot[s], (eng, fname, s, v, self.tot[s])
        for s, n in inc:
            self.tot[s] += n
        wait = tuple(wait)
        inc = tuple(inc)

        def fn(e, sems, fname=fname, a=a, k=k, wait=wait, inc=inc):
            for s, v in wait:
                e.wait_ge(sems[s], v)
            ins = getattr(e, fname)(*a, **k)
            for s, n in inc:
                ins = ins.then_inc(sems[s], n)

        self.q[eng].append(fn)

    def wait(self, eng, sem, val):
        assert val <= self.tot[sem], (eng, sem, val, self.tot[sem])
        if val <= 0:
            return

        def fn(e, sems, sem=sem, val=val):
            e.wait_ge(sems[sem], val)

        self.q[eng].append(fn)

    def barrier(self):
        snap = {s: self.tot[s] for s in ("pe", "act", "dve")}
        own = {"tensor": "pe", "scalar": "act", "vector": "dve"}
        for e in ("tensor", "scalar", "vector"):
            for s, v in snap.items():
                if s != own[e]:
                    self.wait(e, s, v)


def _build(parts=ALL_PARTS):
    nc = bass.Bass("TRN2", target_bir_lowering=False, debug=False)
    nc.detect_race_conditions = False
    S = Sched()

    x_in = nc.dram_tensor("x", [TPC, D], F32, kind="ExternalInput").ap()
    y_out = nc.dram_tensor("y", [TPC, D], F32, kind="ExternalOutput").ap()
    wg_l = nc.dram_tensor("wg", [NIC, 128, D], I8, kind="ExternalInput").ap()
    wu_l = nc.dram_tensor("wu", [NIC, 128, D], I8, kind="ExternalInput").ap()
    wd_l = nc.dram_tensor("wdn", [4, 8, 128, 8 * 512], I8, kind="ExternalInput").ap()
    wq_l = nc.dram_tensor("wq", [16, 128, D], F32, kind="ExternalInput").ap()
    wk_l = nc.dram_tensor("wk", [4, 128, D], F32, kind="ExternalInput").ap()
    wv_l = nc.dram_tensor("wv", [4, 128, D], F32, kind="ExternalInput").ap()
    wo_l = nc.dram_tensor("wo", [16, 128, D], F32, kind="ExternalInput").ap()
    sg_d = nc.dram_tensor("sg", [128, NIC], F32, kind="ExternalInput").ap()
    su_d = nc.dram_tensor("su", [128, NIC], F32, kind="ExternalInput").ap()
    sd_d = nc.dram_tensor("sd", [128, CC], F32, kind="ExternalInput").ap()
    bq_d = nc.dram_tensor("bq", [128, 16], F32, kind="ExternalInput").ap()
    bk_d = nc.dram_tensor("bk", [128, 4], F32, kind="ExternalInput").ap()
    bv_d = nc.dram_tensor("bv", [128, 4], F32, kind="ExternalInput").ap()
    bo_d = nc.dram_tensor("bo", [128, 16], F32, kind="ExternalInput").ap()
    cosq_d = nc.dram_tensor("cosq", [128, 16], F32, kind="ExternalInput").ap()
    sinq_d = nc.dram_tensor("sinq", [128, 16], F32, kind="ExternalInput").ap()
    cosk_d = nc.dram_tensor("cosk", [128, 4], F32, kind="ExternalInput").ap()
    sink_d = nc.dram_tensor("sink", [128, 4], F32, kind="ExternalInput").ap()
    mask_d = nc.dram_tensor("mask", [T, TPC], BF16, kind="ExternalInput").ap()
    idb_d = nc.dram_tensor("idb", [128, 128], BF16, kind="ExternalInput").ap()
    idf_d = nc.dram_tensor("idf", [128, 128], F32, kind="ExternalInput").ap()
    onesm_d = nc.dram_tensor("onesm", [128, 128], BF16, kind="ExternalInput").ap()
    epsv_d = nc.dram_tensor("epsv", [128, 1], F32, kind="ExternalInput").ap()
    wqb_l = nc.dram_tensor("wqb", [16, 128, D], BF16).ap()
    wkb_l = nc.dram_tensor("wkb", [4, 128, D], BF16).ap()
    wvb_l = nc.dram_tensor("wvb", [4, 128, D], BF16).ap()
    wob_l = nc.dram_tensor("wob", [16, 128, D], BF16).ap()
    cc_in = [nc.dram_tensor(f"cc_in{t}", [2 * KV * 128, TTS], BF16) for t in range(2)]
    cc_out = [nc.dram_tensor(f"cc_out{t}", [4 * KV * 128, TTS], BF16) for t in range(2)]

    x_res = nc.alloc_sbuf_tensor("x_res", [128, CC * TPC], F32)
    xln = nc.alloc_sbuf_tensor("xln", [128, CC * TTS], BF16)
    scr = nc.alloc_sbuf_tensor("scr", [128, TTS], F32)
    dtmp = nc.alloc_sbuf_tensor("dtmp", [128, TTS], F32)
    rtmp = nc.alloc_sbuf_tensor("rtmp", [128, TTS], F32)
    sgs = nc.alloc_sbuf_tensor("sgs", [128, NIC], F32)
    sus = nc.alloc_sbuf_tensor("sus", [128, NIC], F32)
    sdh = nc.alloc_sbuf_tensor("sdh", [128, CC], F32)
    bqs = nc.alloc_sbuf_tensor("bqs", [128, 16], F32)
    bks = nc.alloc_sbuf_tensor("bks", [128, 4], F32)
    bvs = nc.alloc_sbuf_tensor("bvs", [128, 4], F32)
    bos = nc.alloc_sbuf_tensor("bos", [128, 16], F32)
    cosq = nc.alloc_sbuf_tensor("cosq_s", [128, 16], F32)
    sinq = nc.alloc_sbuf_tensor("sinq_s", [128, 16], F32)
    cosk = nc.alloc_sbuf_tensor("cosk_s", [128, 4], F32)
    sink = nc.alloc_sbuf_tensor("sink_s", [128, 4], F32)
    idb = nc.alloc_sbuf_tensor("idb_s", [128, 128], BF16)
    idf = nc.alloc_sbuf_tensor("idf_s", [128, 128], F32)
    onesm = nc.alloc_sbuf_tensor("onesm_s", [128, 128], BF16)
    epsv = nc.alloc_sbuf_tensor("epsv_s", [128, 1], F32)

    RB = (nc.sbuf_base + 31) & ~31
    fuse = nc.alloc_sbuf_tensor_at("fuse", [128, NIC * TTS], BF16, offset=RB)
    wslot = nc.alloc_sbuf_tensor_at("wslot", [128, 6 * D], BF16, offset=RB + 64 * KB)
    wdb = nc.alloc_sbuf_tensor_at("wdb", [128, 2 * 8 * 512], BF16, offset=RB + 88 * KB)
    gub = nc.alloc_sbuf_tensor_at("gub", [128, 4 * TTS], BF16, offset=RB + 104 * KB)
    wdb3 = nc.alloc_sbuf_tensor_at("wdb3", [128, 8 * 512], BF16, offset=RB + 104 * KB)
    qp = nc.alloc_sbuf_tensor_at("qp", [128, H * TPC], BF16, offset=RB)
    kf = nc.alloc_sbuf_tensor_at("kf", [128, KV * 2 * TPC], BF16, offset=RB + 32 * KB)
    vt = nc.alloc_sbuf_tensor_at("vt", [128, 16 * TTS], BF16, offset=RB + 48 * KB)
    eT = nc.alloc_sbuf_tensor_at("eT", [128, 4 * TTS], BF16, offset=RB + 80 * KB)
    o_fm = nc.alloc_sbuf_tensor_at("o_fm", [128, H * TTS], BF16, offset=RB + 84 * KB)
    kown = nc.alloc_sbuf_tensor_at("kown", [128, KV * TPC], BF16, offset=RB + 84 * KB)
    vown = nc.alloc_sbuf_tensor_at("vown", [128, 8 * TTS], BF16, offset=RB + 92 * KB)
    xln2 = nc.alloc_sbuf_tensor_at("xln2", [128, CC * TTS], BF16, offset=RB + 100 * KB)
    vstage = nc.alloc_sbuf_tensor_at("vstage", [128, TTS], BF16, offset=RB + 116 * KB)
    sqscr = nc.alloc_sbuf_tensor_at("sqscr", [128, 2 * TTS], BF16, offset=RB + 117 * KB)
    tok = nc.alloc_sbuf_tensor_at("tok", [128, 2 * D], F32, offset=RB)
    REGION_END = RB + 119 * KB
    assert REGION_END <= 229344, (RB, REGION_END)

    pa = nc.alloc_psum_tensor("pa", [128, 2 * TTS], F32)
    pb = nc.alloc_psum_tensor("pb", [128, 2 * TTS], F32)
    pdn = nc.alloc_psum_tensor("pdn", [128, 4 * TTS], F32)

    sems = {s: nc.alloc_semaphore("s_" + s)
            for s in ("pe", "act", "dve", "cc", "cst",
                      "w0", "w1", "w2", "w3", "w4", "w5",
                      "wd0", "wd1", "wd2", "msk", "kv0", "kv1", "rc0", "rc1",
                      "ip0", "ip1", "pcw")}

    xr = x_res.ap().rearrange("p (cc t) -> p cc t", cc=CC)
    xl = xln.ap().rearrange("p (cc t) -> p cc t", cc=CC)
    fu = fuse.ap().rearrange("p (ic t) -> p ic t", ic=NIC)
    ws = wslot.ap().rearrange("p (s o) -> p s o", s=6)
    wdbv = wdb.ap().rearrange("p (b k o) -> p b k o", b=2, k=8)
    wdb3v = wdb3.ap().rearrange("p (k o) -> p k o", k=8)
    qpv = qp.ap().rearrange("p (h t) -> p h t", h=H)
    kfv = kf.ap().rearrange("p (kv s) -> p kv s", kv=KV)
    vtv = vt.ap().rearrange("p (s d) -> p s d", s=16)
    ofv = o_fm.ap().rearrange("p (h t) -> p h t", h=H)
    xl2 = xln2.ap().rearrange("p (cc t) -> p cc t", cc=CC)
    kov = kown.ap().rearrange("p (kv t) -> p kv t", kv=KV)
    vov = vown.ap().rearrange("p (s d) -> p s d", s=8)
    eTv = eT.ap().rearrange("p (s t) -> p s t", s=4)
    tokv = tok.ap()
    masksb = xln.ap().rearrange("p (s t) -> p s t", s=CC)
    sq2 = sqscr.ap().rearrange("p (b t) -> p b t", b=2)
    paa, pba, pdna = pa.ap(), pb.ap(), pdn.ap()
    pdnb = pdn.bitcast(BF16).ap()
    bufstate = {"ws": [("pe", 0)] * 6, "wdb": [("pe", 0)] * 3}

    def bwait(eng, ent):
        S.wait(eng, ent[0], ent[1])

    for dst, src in ((sgs, sg_d), (sus, su_d), (sdh, sd_d), (bqs, bq_d),
                     (bks, bk_d), (bvs, bv_d), (bos, bo_d), (cosq, cosq_d),
                     (sinq, sinq_d), (cosk, cosk_d), (sink, sink_d),
                     (idb, idb_d), (idf, idf_d), (onesm, onesm_d),
                     (epsv, epsv_d)):
        S.op("sync", "dma_start", out=dst.ap(), in_=src, inc=[("cst", 16)])
    consts_io = S.now("cst")
    S.wait("vector", "cst", consts_io)
    S.wait("tensor", "cst", consts_io)
    S.wait("scalar", "cst", consts_io)
    S.op("vector", "tensor_scalar", out=sdh.ap(), in0=sdh.ap(), scalar1=0.5,
         scalar2=None, op0=ALU.mult, inc=[("dve", 1)])

    ev_grp = {}
    ev_tb_pe = {}
    input_evts = {}
    g = 0
    for tb in range(TPC // 128):
        buf = tb % 2
        if tb >= 2:
            S.wait("sync", "pe", ev_tb_pe[tb - 2])
        S.op("sync", "dma_start", out=tokv[:, buf * D:(buf + 1) * D],
             in_=x_in[tb * 128:(tb + 1) * 128, :], inc=[("ip%d" % buf, 16)])
        io_here = S.now("ip%d" % buf)
        for ccg in range(CC // 4):
            par = g % 2
            w = []
            if ccg == 0:
                w.append(("ip%d" % buf, io_here))
            if g >= 2:
                w.append(("act", ev_grp[g - 2]))
            for j in range(4):
                cc = ccg * 4 + j
                S.op("tensor", "transpose",
                     paa[:, par * TTS + j * 128: par * TTS + (j + 1) * 128],
                     tokv[:, buf * D + cc * 128: buf * D + (cc + 1) * 128],
                     idf.ap(),
                     wait=w if j == 0 else (),
                     inc=[("pe", 1)] if j == 3 else [])
            S.op("scalar", "copy",
                 out=xr[:, ccg * 4:(ccg + 1) * 4, tb * 128:(tb + 1) * 128],
                 in_=paa[:, par * TTS:(par + 1) * TTS].rearrange("p (j t) -> p j t", j=4),
                 wait=[("pe", S.now("pe"))], inc=[("act", 1)])
            ev_grp[g] = S.now("act")
            g += 1
        ev_tb_pe[tb] = S.now("pe")
        input_evts[tb] = (S.now("act"), S.now("pe"))

    def ln_stats(tt, ps, cast_wait=(), stats_wait=(), dst=None, r_in_sq2=False):
        dv = xl if dst is None else dst
        tsl = slice(tt * TTS, (tt + 1) * TTS)
        cast_evt = {}
        for cc in range(CC):
            S.op("vector", "tensor_copy", out=dv[:, cc, :], in_=xr[:, cc, tsl],
                 wait=cast_wait if cc == 0 else (), inc=[("dve", 1)])
            cast_evt[cc] = S.now("dve")
        for cc in range(CC):
            w = list(stats_wait) + [("dve", cast_evt[CC - 1])] if cc == 0 else ()
            S.op("tensor", "matmul", ps[:, 0:TTS], lhsT=onesm.ap(),
                 rhs=dv[:, cc, :], start=(cc == 0), stop=(cc == CC - 1), wait=w)
        sq_evt = {}
        sqmm_evt = {}
        for cc in range(2):
            S.op("vector", "tensor_tensor", out=sq2[:, cc, :], in0=dv[:, cc, :],
                 in1=dv[:, cc, :], op=ALU.mult,
                 wait=[("dve", cast_evt[cc])], inc=[("dve", 1)])
            sq_evt[cc] = S.now("dve")
        for cc in range(CC):
            S.op("tensor", "matmul", ps[:, TTS:2 * TTS], lhsT=onesm.ap(),
                 rhs=sq2[:, cc % 2, :], start=(cc == 0), stop=(cc == CC - 1),
                 wait=[("dve", sq_evt[cc])], inc=[("pe", 1)])
            sqmm_evt[cc] = S.now("pe")
            if cc + 2 < CC:
                S.op("vector", "tensor_tensor", out=sq2[:, cc % 2, :],
                     in0=dv[:, cc + 2, :], in1=dv[:, cc + 2, :], op=ALU.mult,
                     wait=[("pe", sqmm_evt[cc]), ("dve", cast_evt[cc + 2])],
                     inc=[("dve", 1)])
                sq_evt[cc + 2] = S.now("dve")
        var_dst = vstage.ap()
        stats_pe = S.now("pe")
        S.op("scalar", "mul", sq2[:, 0, :], ps[:, 0:TTS], 1.0 / D,
             wait=[("pe", stats_pe)], inc=[("act", 1)])
        S.wait("vector", "act", S.now("act"))
        S.op("vector", "tensor_tensor", out=var_dst, in0=sq2[:, 0, :],
             in1=sq2[:, 0, :], op=ALU.mult)
        S.op("vector", "scalar_tensor_tensor", out=var_dst,
             in0=ps[:, TTS:2 * TTS], scalar=1.0 / D, in1=var_dst,
             op0=ALU.mult, op1=ALU.subtract, inc=[("dve", 1)])
        S.wait("scalar", "dve", S.now("dve"))
        S.op("scalar", "activation", out=ps[:, 0:TTS], in_=var_dst,
             func=AF.Ln, bias=epsv.ap(), scale=1.0)
        S.op("scalar", "activation", out=(sq2[:, 1, :] if r_in_sq2 else scr.ap()),
             in_=ps[:, 0:TTS], func=AF.Exp, scale=-0.5, inc=[("act", 1)])
        return S.now("act")

    def ln_apply(tt, r_act, dst=None, r_src=None):
        dv = xl if dst is None else dst
        rv = scr.ap() if r_src is None else r_src
        tsl = slice(tt * TTS, (tt + 1) * TTS)
        S.wait("vector", "act", r_act)
        for cc in range(CC):
            S.op("vector", "tensor_tensor", out=dv[:, cc, :],
                 in0=xr[:, cc, tsl], in1=sq2[:, 0, :], op=ALU.subtract)
            S.op("vector", "tensor_tensor", out=dv[:, cc, :], in0=dv[:, cc, :],
                 in1=rv, op=ALU.mult,
                 inc=[("dve", 1)] if cc == CC - 1 else [])

    def layernorm(tt):
        S.barrier()
        r_act = ln_stats(tt, paa)
        ln_apply(tt, r_act)

    def ffn(tt, ln_next=None, xv=None):
        S.barrier()
        if xv is None:
            xv = xl
        tsl = slice(tt * TTS, (tt + 1) * TTS)
        xln_ready = S.now("dve")
        ev_mm = {}
        ev_gevac = {}
        ev_fm = {}
        wd_g = {}
        wd_u = {}

        def issue_w(ic):
            pr = ic % 3
            bwait("gpsimd", bufstate["ws"][2 * pr])
            bwait("gpsimd", bufstate["ws"][2 * pr + 1])
            S.op("gpsimd", "dma_start", out=ws[:, 2 * pr, :], in_=wg_l[ic],
                 inc=[("w%d" % (2 * pr), 16)])
            wd_g[ic] = ("w%d" % (2 * pr), S.now("w%d" % (2 * pr)))
            S.op("gpsimd", "dma_start", out=ws[:, 2 * pr + 1, :], in_=wu_l[ic],
                 inc=[("w%d" % (2 * pr + 1), 16)])
            wd_u[ic] = ("w%d" % (2 * pr + 1), S.now("w%d" % (2 * pr + 1)))

        issue_w(0)
        issue_w(1)
        issue_w(2)
        for ic in range(NIC):
            bsel = ic % 2
            pr = ic % 3
            for cc in range(CC):
                w = []
                if cc == 0:
                    w = [wd_g[ic], ("dve", xln_ready)]
                    if ic >= 2:
                        w.append(("act", ev_gevac[ic - 2]))
                S.op("tensor", "matmul", paa[:, bsel * TTS:(bsel + 1) * TTS],
                     lhsT=ws[:, 2 * pr, cc * 128:(cc + 1) * 128],
                     rhs=xv[:, cc, :], start=(cc == 0), stop=(cc == CC - 1),
                     wait=w, inc=[("pe", 1)] if cc == CC - 1 else [])
            gate_pe = S.now("pe")
            for cc in range(CC):
                S.op("tensor", "matmul", pba[:, bsel * TTS:(bsel + 1) * TTS],
                     lhsT=ws[:, 2 * pr + 1, cc * 128:(cc + 1) * 128],
                     rhs=xv[:, cc, :], start=(cc == 0), stop=(cc == CC - 1),
                     wait=[wd_u[ic]] if cc == 0 else (),
                     inc=[("pe", 1)] if cc == CC - 1 else [])
            ev_mm[ic] = S.now("pe")
            bufstate["ws"][2 * pr] = bufstate["ws"][2 * pr + 1] = ("pe", ev_mm[ic])
            if ic + 3 < NIC:
                issue_w(ic + 3)
            g_sb = gub.ap()[:, bsel * TTS:(bsel + 1) * TTS]
            u_sb = gub.ap()[:, (2 + bsel) * TTS:(3 + bsel) * TTS]
            w = [("pe", gate_pe)]
            if ic >= 2:
                w.append(("dve", ev_fm[ic - 2]))
            S.op("scalar", "activation", out=g_sb,
                 in_=paa[:, bsel * TTS:(bsel + 1) * TTS], func=AF.Silu,
                 scale=sgs.ap()[:, ic:ic + 1], wait=w, inc=[("act", 1)])
            S.op("scalar", "activation", out=u_sb,
                 in_=pba[:, bsel * TTS:(bsel + 1) * TTS], func=AF.Copy,
                 scale=sus.ap()[:, ic:ic + 1], wait=[("pe", ev_mm[ic])],
                 inc=[("act", 1)])
            ev_gevac[ic] = S.now("act")
            S.op("vector", "tensor_tensor", out=fu[:, ic, :], in0=g_sb,
                 in1=u_sb, op=ALU.mult, wait=[("act", ev_gevac[ic])],
                 inc=[("dve", 1)])
            ev_fm[ic] = S.now("dve")
        fuse_done = S.now("dve")
        ev_dmm = {}
        ev_bank = {}
        ev_add = {}
        bufstate["wdb"][2] = ("dve", fuse_done)
        for q in range(4):
            for io in range(8):
                bsel = (q * 8 + io) % 3
                bwait("gpsimd", bufstate["wdb"][bsel])
                S.op("gpsimd", "dma_start",
                     out=(wdbv[:, bsel, :, :] if bsel < 2 else wdb3v),
                     in_=wd_l[q, io].rearrange("p (k o) -> p k o", k=8),
                     inc=[("wd%d" % bsel, 16)])
                wready = ("wd%d" % bsel, S.now("wd%d" % bsel))
                for k in range(8):
                    for o in range(4):
                        w = []
                        if k == 0 and o == 0:
                            w = [wready]
                            if io == 0:
                                w.append(("dve", fuse_done))
                        if k == 0 and io == 0 and q > 0:
                            w.append(("act", ev_bank[(q - 1, o)]))
                        S.op("tensor", "matmul", pdna[:, o * TTS:(o + 1) * TTS],
                             lhsT=(wdbv[:, bsel, k, o * 128:(o + 1) * 128]
                                   if bsel < 2 else wdb3v[:, k, o * 128:(o + 1) * 128]),
                             rhs=fu[:, io * 8 + k, :],
                             start=(io == 0 and k == 0), stop=(io == 7 and k == 7),
                             wait=w,
                             inc=[("pe", 1)] if (k == 7 and o == 3) else [])
                ev_dmm[(q, io)] = S.now("pe")
                bufstate["wdb"][bsel] = ("pe", ev_dmm[(q, io)])
                if q == 0 and io == 0 and ln_next is not None:
                    r_act = ln_stats(ln_next, paa,
                                     cast_wait=[("pe", ev_mm[NIC - 1])],
                                     stats_wait=[("act", ev_gevac[NIC - 1])])
                    ln_apply(ln_next, r_act)
            qmm_pe = S.now("pe")
            for o in range(4):
                oc = q * 4 + o
                gi = q * 4 + o
                buf = dtmp if gi % 2 == 0 else rtmp
                w = [("pe", qmm_pe)]
                if (q, o - 2) in ev_add:
                    w.append(("dve", ev_add[(q, o - 2)]))
                elif q > 0 and o < 2:
                    w.append(("dve", ev_add[(q - 1, o + 2)]))
                S.op("scalar", "activation", out=buf.ap(),
                     in_=pdna[:, o * TTS:(o + 1) * TTS], func=AF.Copy,
                     scale=sdh.ap()[:, oc:oc + 1], wait=w, inc=[("act", 1)])
                ev_bank[(q, o)] = S.now("act")
                S.op("vector", "tensor_tensor", out=xr[:, oc, tsl],
                     in0=xr[:, oc, tsl], in1=buf.ap(), op=ALU.add,
                     wait=[("act", ev_bank[(q, o)])], inc=[("dve", 1)])
                ev_add[(q, o)] = S.now("dve")

    qkv_state = {"i": 0, "vtr_dve": 0, "ev_evac": [0, 0], "xln_ready": 0}

    def qkv(tt, kinds, xv=None, bar=True):
        if bar:
            S.barrier()
        if xv is None:
            xv = xl
        xln_ready = S.now("dve")
        ev_slot = {}
        ev_evac = {}
        state = qkv_state
        state["xln_ready"] = xln_ready

        def proj(kind, n_oc, w_dram, bias_sb):
            xln_ready = state["xln_ready"]
            for oc in range(n_oc):
                slot = state["i"] % 5
                par = state["i"] % 2
                bwait("gpsimd", bufstate["ws"][slot])
                S.op("gpsimd", "dma_start", out=ws[:, slot, :], in_=w_dram[oc],
                     inc=[("w%d" % slot, 16)])
                wready = ("w%d" % slot, S.now("w%d" % slot))
                for cc in range(CC):
                    w = []
                    if cc == 0:
                        w = [wready, ("dve", xln_ready)]
                        if state["ev_evac"][par]:
                            w.append(("act", state["ev_evac"][par]))
                    S.op("tensor", "matmul", paa[:, par * TTS:(par + 1) * TTS],
                         lhsT=ws[:, slot, cc * 128:(cc + 1) * 128],
                         rhs=xv[:, cc, :], start=(cc == 0), stop=(cc == CC - 1),
                         wait=w, inc=[("pe", 1)] if cc == CC - 1 else [])
                ev_slot[slot] = S.now("pe")
                bufstate["ws"][slot] = ("pe", ev_slot[slot])
                psrc = paa[:, par * TTS:(par + 1) * TTS]
                if kind == "v":
                    S.op("scalar", "activation", out=vstage.ap(), in_=psrc,
                         func=AF.Identity, bias=bias_sb[:, oc:oc + 1], scale=1.0,
                         wait=[("pe", ev_slot[slot]), ("dve", state["vtr_dve"])],
                         inc=[("act", 1)])
                    state["ev_evac"][par] = S.now("act")
                    for j in range(4):
                        w = []
                        if j == 0:
                            w = [("act", state["ev_evac"][par]), ("dve", state["vtr_dve"])]
                        S.op("tensor", "transpose",
                             pdnb[:, j * 128:(j + 1) * 128],
                             vstage.ap()[:, j * 128:(j + 1) * 128], idb.ap(),
                             wait=w, inc=[("pe", 1)] if j == 3 else [])
                    S.op("vector", "tensor_copy",
                         out=vov[:, tt * 4:(tt + 1) * 4, oc * 128:(oc + 1) * 128],
                         in_=pdnb[:, 0:TTS].rearrange("p (j t) -> p j t", j=4),
                         wait=[("pe", S.now("pe"))], inc=[("dve", 1)])
                    state["vtr_dve"] = S.now("dve")
                else:
                    S.op("scalar", "activation", out=rtmp.ap(), in_=psrc,
                         func=AF.Identity, bias=bias_sb[:, oc:oc + 1], scale=1.0,
                         wait=[("pe", ev_slot[slot]), ("dve", S.now("dve"))],
                         inc=[("act", 1)])
                    state["ev_evac"][par] = S.now("act")
                    cosv = (cosq if kind == "q" else cosk).ap()[:, oc:oc + 1]
                    sinv = (sinq if kind == "q" else sink).ap()[:, oc:oc + 1]
                    dest = (qpv[:, oc, tt * TTS:(tt + 1) * TTS] if kind == "q"
                            else kov[:, oc, tt * TTS:(tt + 1) * TTS])
                    S.wait("vector", "act", state["ev_evac"][par])
                    S.op("vector", "tensor_copy", out=scr.ap()[0:64, :],
                         in_=rtmp.ap()[64:128, :])
                    S.op("vector", "tensor_copy", out=scr.ap()[64:128, :],
                         in_=rtmp.ap()[0:64, :])
                    S.op("vector", "tensor_scalar", out=scr.ap(), in0=scr.ap(),
                         scalar1=sinv, scalar2=None, op0=ALU.mult)
                    S.op("vector", "scalar_tensor_tensor", out=dest,
                         in0=rtmp.ap(), scalar=cosv, in1=scr.ap(),
                         op0=ALU.mult, op1=ALU.add, inc=[("dve", 1)])
                state["i"] += 1

        for kind in kinds:
            if kind == "k":
                proj("k", KV, wk_l, bks.ap())
            elif kind == "v":
                proj("v", KV, wv_l, bvs.ap())
            else:
                proj("q", H, wq_l, bqs.ap())

    def exchange_send(tt):
        dve_done = S.now("dve")
        S.wait("sync", "dve", dve_done)
        ksem = "kv%d" % tt
        S.op("sync", "dma_start",
             out=cc_in[tt].ap()[0:KV * 128, :].rearrange("(kv p) t -> p kv t", p=128),
             in_=kov[:, :, tt * TTS:(tt + 1) * TTS], inc=[(ksem, 16)])
        S.op("sync", "dma_start",
             out=cc_in[tt].ap()[KV * 128:2 * KV * 128, :].rearrange("(s p) d -> p s d", p=128),
             in_=vov[:, tt * 4:(tt + 1) * 4, :], inc=[(ksem, 16)])
        S.wait("gpsimd", ksem, S.now(ksem))
        S.op("gpsimd", "collective_compute", "AllGather", ALU.bypass,
             replica_groups=[[0, 1], [2, 3], [4, 5], [6, 7]],
             ins=[cc_in[tt].ap().opt()], outs=[cc_out[tt].ap().opt()],
             inc=[("cc", 1)])

    def exchange_recv():
        ios = []
        for tt in range(2):
            q = "sync" if tt == 0 else "gpsimd"
            rsem = "rc%d" % tt
            S.wait(q, "cc", tt + 1)
            for r in range(2):
                base = r * 2 * KV * 128
                S.op(q, "dma_start",
                     out=kfv[:, :, r * TPC + tt * TTS: r * TPC + (tt + 1) * TTS],
                     in_=cc_out[tt].ap()[base:base + KV * 128, :]
                         .rearrange("(kv p) t -> p kv t", p=128),
                     inc=[(rsem, 16)])
                S.op(q, "dma_start",
                     out=vtv[:, r * 8 + tt * 4: r * 8 + (tt + 1) * 4, :],
                     in_=cc_out[tt].ap()[base + KV * 128:base + 2 * KV * 128, :]
                         .rearrange("(s p) d -> p s d", p=128),
                     inc=[(rsem, 16)])
            ios.append((rsem, S.now(rsem)))
        return ios

    def load_mask(tt, free_pe):
        S.wait("sync", "pe", free_pe)
        S.op("sync", "dma_start",
             out=masksb,
             in_=mask_d[:, tt * TTS:(tt + 1) * TTS].rearrange("(s p) t -> p s t", p=128),
             inc=[("msk", 16)])
        return S.now("msk")

    SIG = (0, 1, 2, 3, 8, 9, 10, 11, 4, 5, 6, 7, 12, 13, 14, 15)

    def scores(tt, kv_io0, kv_io1, mask_io):
        S.barrier()
        pslot = [paa[:, 0:TTS], paa[:, TTS:2 * TTS],
                 pdna[:, 2 * TTS:3 * TTS], pdna[:, 3 * TTS:4 * TTS]]
        ppair = [paa[:, 0:2 * TTS], pdna[:, 2 * TTS:4 * TTS]]
        exp_evt = {}
        avz_evt = {}
        out_dve = [0, 0]
        z_act = [0, 0]

        def emit_avz(h, j, first_banks_w=()):
            kv = h // GRP
            s = SIG[j]
            pz = pba if h % 2 == 0 else pdna
            w = [("act", exp_evt[(h, j)])]
            if j == 0:
                w += list(first_banks_w)
            S.op("tensor", "matmul", pz[:, 0:TTS],
                 lhsT=vtv[:, s, kv * 128:(kv + 1) * 128], rhs=eTv[:, j % 4, :],
                 start=(j == 0), stop=(j == 15), wait=w)
            S.op("tensor", "matmul", pz[:, TTS:2 * TTS],
                 lhsT=onesm.ap(), rhs=eTv[:, j % 4, :],
                 start=(j == 0), stop=(j == 15), inc=[("pe", 1)])
            avz_evt[(h, j)] = S.now("pe")

        def emit_norm(h):
            pz = pba if h % 2 == 0 else pdna
            zs = dtmp if h % 2 == 0 else rtmp
            S.op("scalar", "activation", out=scr.ap(), in_=pz[:, TTS:2 * TTS],
                 func=AF.Ln, wait=[("pe", avz_evt[(h, 15)])])
            S.op("scalar", "activation", out=zs.ap(), in_=scr.ap(), func=AF.Exp,
                 scale=-1.0, inc=[("act", 1)])
            z_act[h % 2] = S.now("act")
            S.op("vector", "tensor_tensor",
                 out=ofv[:, h, :],
                 in0=pz[:, 0:TTS], in1=zs.ap(), op=ALU.mult,
                 wait=[("act", S.now("act"))], inc=[("dve", 1)])
            out_dve[h % 2] = S.now("dve")

        for h in range(H):
            kv = h // GRP
            qtile = qpv[:, h, tt * TTS:(tt + 1) * TTS]
            for j in range(16):
                s = SIG[j]
                w = []
                if h == 0 and j == 0:
                    w = [kv_io0, ("msk", mask_io)]
                if h == 0 and j == 8:
                    w = [kv_io1]
                key = (h, j - 4) if j >= 4 else ((h - 1, 12 + j) if h > 0 else None)
                if key is not None:
                    w.append(("act", exp_evt[key]))
                S.op("tensor", "matmul", pslot[j % 4],
                     lhsT=kfv[:, kv, s * 128:(s + 1) * 128], rhs=qtile,
                     start=True, stop=False, wait=w)
                S.op("tensor", "matmul", pslot[j % 4],
                     lhsT=idb.ap(), rhs=masksb[:, s, :],
                     start=False, stop=True, inc=[("pe", 1)])
                score_pe = S.now("pe")
                key = (h, j - 4) if j >= 4 else ((h - 1, 12 + j) if h > 0 else None)
                w = [("pe", score_pe)]
                if key is not None:
                    w.append(("pe", avz_evt[key]))
                S.op("scalar", "activation", out=eTv[:, j % 4, :],
                     in_=pslot[j % 4], func=AF.Exp, scale=1.0,
                     wait=w, inc=[("act", 1)])
                exp_evt[(h, j)] = S.now("act")
                if j >= 3:
                    fb = ()
                    if j == 3:
                        fb = [("dve", out_dve[h % 2]), ("act", z_act[h % 2])]
                    emit_avz(h, j - 3, fb)
            emit_avz(h, 13)
            emit_avz(h, 14)
            emit_avz(h, 15)
            emit_norm(h)
        bufstate["ws"][4] = ("pe", S.now("pe"))
        return S.now("pe")

    def oproj(tt, ln_next=None):
        S.barrier()
        of_ready = S.now("dve")
        ev_slot = {}
        ev_evac = {}
        for oc in range(16):
            if oc == 8 and ln_next is not None:
                r_act = ln_stats(ln_next, pba)
                ln_apply(ln_next, r_act)
            slot = oc % 4
            par = oc % 2
            bwait("gpsimd", bufstate["ws"][slot])
            S.op("gpsimd", "dma_start", out=ws[:, slot, :], in_=wo_l[oc],
                 inc=[("w%d" % slot, 16)])
            wready = ("w%d" % slot, S.now("w%d" % slot))
            for cc in range(CC):
                w = []
                if cc == 0:
                    w = [wready, ("dve", of_ready)]
                    if oc >= 2:
                        w.append(("act", ev_evac[par]))
                S.op("tensor", "matmul", paa[:, par * TTS:(par + 1) * TTS],
                     lhsT=ws[:, slot, cc * 128:(cc + 1) * 128],
                     rhs=ofv[:, cc, :],
                     start=(cc == 0), stop=(cc == CC - 1), wait=w,
                     inc=[("pe", 1)] if cc == CC - 1 else [])
            ev_slot[slot] = S.now("pe")
            bufstate["ws"][slot] = ("pe", ev_slot[slot])
            if oc == 15:
                bufstate["ws"][5] = ("pe", ev_slot[slot])
                bufstate["wdb"][0] = ("pe", ev_slot[slot])
                bufstate["wdb"][1] = ("pe", ev_slot[slot])
            S.op("scalar", "activation", out=dtmp.ap(),
                 in_=paa[:, par * TTS:(par + 1) * TTS], func=AF.Identity,
                 bias=bos.ap()[:, oc:oc + 1], scale=1.0,
                 wait=[("pe", ev_slot[slot]), ("dve", S.now("dve"))],
                 inc=[("act", 1)])
            ev_evac[par] = S.now("act")
            S.op("vector", "tensor_tensor",
                 out=xr[:, oc, tt * TTS:(tt + 1) * TTS],
                 in0=xr[:, oc, tt * TTS:(tt + 1) * TTS], in1=dtmp.ap(),
                 op=ALU.add, wait=[("act", ev_evac[par])], inc=[("dve", 1)])

    have1, haveA, have2 = ("ffn1" in parts), ("attn" in parts), ("ffn2" in parts)
    EMB = EMBED_LN
    if parts:
        r0 = ln_stats(0, pba, cast_wait=[("act", input_evts[3][0])],
                      stats_wait=[("pe", input_evts[3][1])])
        ln_apply(0, r0)
    if have1:
        ffn(0, ln_next=1 if EMB else None)
        if not EMB:
            layernorm(1)
        ffn(1, ln_next=(0 if (haveA or have2) else None) if EMB else None)
    if haveA:
        if have1 and not EMB:
            layernorm(0)
        qkv(0, ("k", "v"))
        exchange_send(0)
        r21 = ln_stats(1, pba, dst=xl2, r_in_sq2=True)
        ln_apply(1, r21, dst=xl2, r_src=sq2[:, 1, :])
        qkv(0, ("q",), bar=False)
        q0_pe = S.now("pe")
        qkv(1, ("k", "v"), xv=xl2, bar=False)
        exchange_send(1)
        qkv(1, ("q",), xv=xl2, bar=False)
        mask_io0 = load_mask(0, q0_pe)
        kv_io0, kv_io1 = exchange_recv()
        mfree = scores(0, kv_io0, kv_io1, mask_io0)
        oproj(0)
        mask_io1 = load_mask(1, mfree)
        scores(1, kv_io0, kv_io1, mask_io1)
        oproj(1, ln_next=0 if (have2 and EMB) else None)
    if have2:
        if (have1 or haveA) and not EMB:
            layernorm(0)
        ffn(0, ln_next=1 if EMB else None)
        if not EMB:
            layernorm(1)
        ffn(1, ln_next=None)

    S.barrier()
    ev_grp = {}
    ev_dma = {}
    g = 0
    first = True
    for tb in range(TPC // 128):
        buf = tb % 2
        for ccg in range(CC // 4):
            par = g % 2
            w = []
            if first:
                w = [("dve", S.now("dve")), ("act", S.now("act"))]
                first = False
            if g >= 2:
                w.append(("act", ev_grp[g - 2]))
            for j in range(4):
                cc = ccg * 4 + j
                S.op("tensor", "transpose",
                     paa[:, par * TTS + j * 128: par * TTS + (j + 1) * 128],
                     xr[:, cc, tb * 128:(tb + 1) * 128], idf.ap(),
                     wait=w if j == 0 else (),
                     inc=[("pe", 1)] if j == 3 else [])
            wloc = [("pe", S.now("pe"))]
            if ccg == 0 and tb >= 2:
                wloc.append(("ip%d" % buf, ev_dma[tb - 2]))
            S.op("scalar", "copy",
                 out=tokv[:, buf * D + ccg * 512: buf * D + (ccg + 1) * 512],
                 in_=paa[:, par * TTS:(par + 1) * TTS],
                 wait=wloc, inc=[("act", 1)])
            ev_grp[g] = S.now("act")
            g += 1
        S.op("sync", "dma_start", out=y_out[tb * 128:(tb + 1) * 128, :],
             in_=tokv[:, buf * D:(buf + 1) * D],
             wait=[("act", S.now("act"))], inc=[("ip%d" % buf, 16)])
        ev_dma[tb] = S.now("ip%d" % buf)
    S.wait("sync", "ip0", S.now("ip0"))
    S.wait("sync", "ip1", S.now("ip1"))

    with nc.Block() as block:
        @block.tensor
        def _(e):
            for fn in S.q["tensor"]:
                fn(e, sems)

        @block.scalar
        def _(e):
            for fn in S.q["scalar"]:
                fn(e, sems)

        @block.vector
        def _(e):
            for fn in S.q["vector"]:
                fn(e, sems)

        @block.sync
        def _(e):
            for fn in S.q["sync"]:
                fn(e, sems)

        @block.gpsimd
        def _(e):
            for fn in S.q["gpsimd"]:
                fn(e, sems)

    return nc


def _host_prep(inputs):
    gi = lambda k: np.ascontiguousarray(np.asarray(inputs[k]))
    x = gi("L0_attn_x")
    wq = gi("L0_attn_wq"); bq = gi("L0_attn_bq")
    wk = gi("L0_attn_wk"); bk = gi("L0_attn_bk")
    wv = gi("L0_attn_wv"); bv = gi("L0_attn_bv")
    wo = gi("L0_attn_wo"); bo = gi("L0_attn_bo")
    wg = gi("L0_ffn_w_gate"); wu = gi("L0_ffn_w_up"); wd = gi("L0_ffn_w_down")
    sgv = gi("L0_ffn_s_gate"); suv = gi("L0_ffn_s_up"); sdv = gi("L0_ffn_s_down")

    def tile_oc(w, noc):
        return np.ascontiguousarray(
            w.reshape(noc, 128, CC, 128).transpose(0, 3, 2, 1).reshape(noc, 128, D))

    wg_l = np.ascontiguousarray(
        wg.reshape(CC, 128, NIC, 128).transpose(2, 1, 0, 3).reshape(NIC, 128, D))
    wu_l = np.ascontiguousarray(
        wu.reshape(CC, 128, NIC, 128).transpose(2, 1, 0, 3).reshape(NIC, 128, D))
    wd_l = np.ascontiguousarray(
        wd.reshape(8, 8, 128, 4, 512).transpose(3, 0, 2, 1, 4).reshape(4, 8, 128, 8 * 512))

    pp = lambda v, n: np.ascontiguousarray(v.reshape(n, 128).T.astype(np.float32))
    consts = {
        "wq": tile_oc(wq, 16), "wk": tile_oc(wk, 4), "wv": tile_oc(wv, 4),
        "wo": tile_oc(wo, 16), "wg": wg_l, "wu": wu_l, "wdn": wd_l,
        "sg": pp(sgv, NIC), "su": pp(suv, NIC), "sd": pp(sdv, CC),
        "bq": pp(bq, 16), "bk": pp(bk, 4), "bv": pp(bv, 4), "bo": pp(bo, 16),
    }
    j = np.arange(64, dtype=np.float64)
    inv = 1.0 / (10000.0 ** (2.0 * j / 128.0))
    scl = 1.0 / np.sqrt(128.0)
    for name, nh, s in (("q", 16, scl), ("k", 4, 1.0)):
        hh = np.arange(nh, dtype=np.float64)
        ang = inv[:, None] * hh[None, :]
        consts["cos" + name] = np.ascontiguousarray(
            (np.concatenate([np.cos(ang), np.cos(ang)], 0) * s).astype(np.float32))
        consts["sin" + name] = np.ascontiguousarray(
            (np.concatenate([-np.sin(ang), np.sin(ang)], 0) * s).astype(np.float32))
    consts["idb"] = np.eye(128, dtype=ml_dtypes.bfloat16)
    consts["idf"] = np.eye(128, dtype=np.float32)
    consts["onesm"] = np.ones((128, 128), dtype=ml_dtypes.bfloat16)
    consts["epsv"] = np.full((128, 1), EPS, dtype=np.float32)

    maskT = np.tril(np.ones((T, T), dtype=np.float32)).T.astype(ml_dtypes.bfloat16)
    in_maps = []
    for c in range(8):
        b, half = c // 2, c % 2
        m = dict(consts)
        m["x"] = np.ascontiguousarray(x[b, half * TPC:(half + 1) * TPC, :])
        m["mask"] = np.ascontiguousarray(maskT[:, half * TPC:(half + 1) * TPC])
        in_maps.append(m)
    return in_maps


def kernel(**inputs):
    nc = _build(ALL_PARTS)
    in_maps = _host_prep(inputs)
    res = run_bass_kernel_spmd(nc, in_maps, core_ids=list(range(8)))
    out = np.empty((B, T, D), np.float32)
    for c in range(8):
        b, half = c // 2, c % 2
        out[b, half * TPC:(half + 1) * TPC, :] = res.results[c]["y"]
    return out


# version 21
# speedup vs baseline: 1.1238x; 1.1238x over previous
import numpy as np
import ml_dtypes
import concourse.bass as bass
import concourse.mybir as mybir
from concourse.bass_utils import run_bass_kernel_spmd

F32 = mybir.dt.float32
BF16 = mybir.dt.bfloat16
I8 = mybir.dt.int8
AF = mybir.ActivationFunctionType
ALU = mybir.AluOpType

B, T, D = 4, 2048, 2048
H, KV, HD = 16, 4, 128
GRP = H // KV
INTER = 8192
TPC = 1024
TTS = 512
NTT = TPC // TTS
CC = D // 128
NIC = INTER // 128
EPS = 1e-5
KB = 1024

ALL_PARTS = ("ffn1", "attn", "ffn2")
EMBED_LN = True


class Sched:

    def __init__(self):
        self.q = {e: [] for e in ("tensor", "scalar", "vector", "sync", "gpsimd")}
        self.tot = {s: 0 for s in
                    ("pe", "act", "dve", "cc", "cst",
                     "w0", "w1", "w2", "w3", "w4", "w5",
                     "wd0", "wd1", "msk", "kv0", "kv1", "rc0", "rc1",
                     "ip0", "ip1", "pcw")}

    def now(self, sem):
        return self.tot[sem]

    def op(self, eng, fname, *a, wait=(), inc=(), **k):
        wait = [w for w in wait if w is not None and w[1] > 0]
        for s, v in wait:
            assert v <= self.tot[s], (eng, fname, s, v, self.tot[s])
        for s, n in inc:
            self.tot[s] += n
        wait = tuple(wait)
        inc = tuple(inc)

        def fn(e, sems, fname=fname, a=a, k=k, wait=wait, inc=inc):
            for s, v in wait:
                e.wait_ge(sems[s], v)
            ins = getattr(e, fname)(*a, **k)
            for s, n in inc:
                ins = ins.then_inc(sems[s], n)

        self.q[eng].append(fn)

    def wait(self, eng, sem, val):
        assert val <= self.tot[sem], (eng, sem, val, self.tot[sem])
        if val <= 0:
            return

        def fn(e, sems, sem=sem, val=val):
            e.wait_ge(sems[sem], val)

        self.q[eng].append(fn)

    def barrier(self):
        snap = {s: self.tot[s] for s in ("pe", "act", "dve")}
        own = {"tensor": "pe", "scalar": "act", "vector": "dve"}
        for e in ("tensor", "scalar", "vector"):
            for s, v in snap.items():
                if s != own[e]:
                    self.wait(e, s, v)


def _build(parts=ALL_PARTS):
    nc = bass.Bass("TRN2", target_bir_lowering=False, debug=False)
    nc.detect_race_conditions = False
    S = Sched()

    x_in = nc.dram_tensor("x", [TPC, D], F32, kind="ExternalInput").ap()
    y_out = nc.dram_tensor("y", [TPC, D], F32, kind="ExternalOutput").ap()
    wg_l = nc.dram_tensor("wg", [NIC, 128, D], I8, kind="ExternalInput").ap()
    wu_l = nc.dram_tensor("wu", [NIC, 128, D], I8, kind="ExternalInput").ap()
    wd_l = nc.dram_tensor("wdn", [4, 8, 128, 8 * 512], I8, kind="ExternalInput").ap()
    wq_l = nc.dram_tensor("wq", [16, 128, D], F32, kind="ExternalInput").ap()
    wk_l = nc.dram_tensor("wk", [4, 128, D], F32, kind="ExternalInput").ap()
    wv_l = nc.dram_tensor("wv", [4, 128, D], F32, kind="ExternalInput").ap()
    wo_l = nc.dram_tensor("wo", [16, 128, D], F32, kind="ExternalInput").ap()
    sg_d = nc.dram_tensor("sg", [128, NIC], F32, kind="ExternalInput").ap()
    su_d = nc.dram_tensor("su", [128, NIC], F32, kind="ExternalInput").ap()
    sd_d = nc.dram_tensor("sd", [128, CC], F32, kind="ExternalInput").ap()
    bq_d = nc.dram_tensor("bq", [128, 16], F32, kind="ExternalInput").ap()
    bk_d = nc.dram_tensor("bk", [128, 4], F32, kind="ExternalInput").ap()
    bv_d = nc.dram_tensor("bv", [128, 4], F32, kind="ExternalInput").ap()
    bo_d = nc.dram_tensor("bo", [128, 16], F32, kind="ExternalInput").ap()
    cosq_d = nc.dram_tensor("cosq", [128, 16], F32, kind="ExternalInput").ap()
    sinq_d = nc.dram_tensor("sinq", [128, 16], F32, kind="ExternalInput").ap()
    cosk_d = nc.dram_tensor("cosk", [128, 4], F32, kind="ExternalInput").ap()
    sink_d = nc.dram_tensor("sink", [128, 4], F32, kind="ExternalInput").ap()
    mask_d = nc.dram_tensor("mask", [T, TPC], BF16, kind="ExternalInput").ap()
    idb_d = nc.dram_tensor("idb", [128, 128], BF16, kind="ExternalInput").ap()
    idf_d = nc.dram_tensor("idf", [128, 128], F32, kind="ExternalInput").ap()
    onesm_d = nc.dram_tensor("onesm", [128, 128], BF16, kind="ExternalInput").ap()
    epsv_d = nc.dram_tensor("epsv", [128, 1], F32, kind="ExternalInput").ap()
    wqb_l = nc.dram_tensor("wqb", [16, 128, D], BF16).ap()
    wkb_l = nc.dram_tensor("wkb", [4, 128, D], BF16).ap()
    wvb_l = nc.dram_tensor("wvb", [4, 128, D], BF16).ap()
    wob_l = nc.dram_tensor("wob", [16, 128, D], BF16).ap()
    cc_in = [nc.dram_tensor(f"cc_in{t}", [2 * KV * 128, TTS], BF16) for t in range(2)]
    cc_out = [nc.dram_tensor(f"cc_out{t}", [4 * KV * 128, TTS], BF16) for t in range(2)]

    x_res = nc.alloc_sbuf_tensor("x_res", [128, CC * TPC], F32)
    xln = nc.alloc_sbuf_tensor("xln", [128, CC * TTS], BF16)
    scr = nc.alloc_sbuf_tensor("scr", [128, TTS], F32)
    dtmp = nc.alloc_sbuf_tensor("dtmp", [128, TTS], F32)
    rtmp = nc.alloc_sbuf_tensor("rtmp", [128, TTS], F32)
    sgs = nc.alloc_sbuf_tensor("sgs", [128, NIC], F32)
    sus = nc.alloc_sbuf_tensor("sus", [128, NIC], F32)
    sdh = nc.alloc_sbuf_tensor("sdh", [128, CC], F32)
    bqs = nc.alloc_sbuf_tensor("bqs", [128, 16], F32)
    bks = nc.alloc_sbuf_tensor("bks", [128, 4], F32)
    bvs = nc.alloc_sbuf_tensor("bvs", [128, 4], F32)
    bos = nc.alloc_sbuf_tensor("bos", [128, 16], F32)
    cosq = nc.alloc_sbuf_tensor("cosq_s", [128, 16], F32)
    sinq = nc.alloc_sbuf_tensor("sinq_s", [128, 16], F32)
    cosk = nc.alloc_sbuf_tensor("cosk_s", [128, 4], F32)
    sink = nc.alloc_sbuf_tensor("sink_s", [128, 4], F32)
    idb = nc.alloc_sbuf_tensor("idb_s", [128, 128], BF16)
    idf = nc.alloc_sbuf_tensor("idf_s", [128, 128], F32)
    onesm = nc.alloc_sbuf_tensor("onesm_s", [128, 128], BF16)
    epsv = nc.alloc_sbuf_tensor("epsv_s", [128, 1], F32)

    RB = (nc.sbuf_base + 31) & ~31
    fuse = nc.alloc_sbuf_tensor_at("fuse", [128, NIC * TTS], BF16, offset=RB)
    wslot = nc.alloc_sbuf_tensor_at("wslot", [128, 6 * D], BF16, offset=RB + 64 * KB)
    wdb = nc.alloc_sbuf_tensor_at("wdb", [128, 2 * 8 * 512], BF16, offset=RB + 88 * KB)
    gub = nc.alloc_sbuf_tensor_at("gub", [128, 4 * TTS], BF16, offset=RB + 104 * KB)
    qp = nc.alloc_sbuf_tensor_at("qp", [128, H * TPC], BF16, offset=RB)
    kf = nc.alloc_sbuf_tensor_at("kf", [128, KV * 2 * TPC], BF16, offset=RB + 32 * KB)
    vt = nc.alloc_sbuf_tensor_at("vt", [128, 16 * TTS], BF16, offset=RB + 48 * KB)
    eT = nc.alloc_sbuf_tensor_at("eT", [128, 4 * TTS], BF16, offset=RB + 80 * KB)
    o_fm = nc.alloc_sbuf_tensor_at("o_fm", [128, H * TTS], BF16, offset=RB + 84 * KB)
    kown = nc.alloc_sbuf_tensor_at("kown", [128, KV * TPC], BF16, offset=RB + 84 * KB)
    vown = nc.alloc_sbuf_tensor_at("vown", [128, 8 * TTS], BF16, offset=RB + 92 * KB)
    xln2 = nc.alloc_sbuf_tensor_at("xln2", [128, CC * TTS], BF16, offset=RB + 100 * KB)
    vstage = nc.alloc_sbuf_tensor_at("vstage", [128, TTS], BF16, offset=RB + 116 * KB)
    sqscr = nc.alloc_sbuf_tensor_at("sqscr", [128, 2 * TTS], BF16, offset=RB + 117 * KB)
    tok = nc.alloc_sbuf_tensor_at("tok", [128, 2 * D], F32, offset=RB)
    REGION_END = RB + 119 * KB
    assert REGION_END <= 229344, (RB, REGION_END)

    pa = nc.alloc_psum_tensor("pa", [128, 2 * TTS], F32)
    pb = nc.alloc_psum_tensor("pb", [128, 2 * TTS], F32)
    pdn = nc.alloc_psum_tensor("pdn", [128, 4 * TTS], F32)

    sems = {s: nc.alloc_semaphore("s_" + s)
            for s in ("pe", "act", "dve", "cc", "cst",
                      "w0", "w1", "w2", "w3", "w4", "w5",
                      "wd0", "wd1", "msk", "kv0", "kv1", "rc0", "rc1",
                      "ip0", "ip1", "pcw")}

    xr = x_res.ap().rearrange("p (cc t) -> p cc t", cc=CC)
    xl = xln.ap().rearrange("p (cc t) -> p cc t", cc=CC)
    fu = fuse.ap().rearrange("p (ic t) -> p ic t", ic=NIC)
    ws = wslot.ap().rearrange("p (s o) -> p s o", s=6)
    wdbv = wdb.ap().rearrange("p (b k o) -> p b k o", b=2, k=8)
    qpv = qp.ap().rearrange("p (h t) -> p h t", h=H)
    kfv = kf.ap().rearrange("p (kv s) -> p kv s", kv=KV)
    vtv = vt.ap().rearrange("p (s d) -> p s d", s=16)
    ofv = o_fm.ap().rearrange("p (h t) -> p h t", h=H)
    xl2 = xln2.ap().rearrange("p (cc t) -> p cc t", cc=CC)
    kov = kown.ap().rearrange("p (kv t) -> p kv t", kv=KV)
    vov = vown.ap().rearrange("p (s d) -> p s d", s=8)
    eTv = eT.ap().rearrange("p (s t) -> p s t", s=4)
    tokv = tok.ap()
    masksb = xln.ap().rearrange("p (s t) -> p s t", s=CC)
    sq2 = sqscr.ap().rearrange("p (b t) -> p b t", b=2)
    paa, pba, pdna = pa.ap(), pb.ap(), pdn.ap()
    pdnb = pdn.bitcast(BF16).ap()
    bufstate = {"ws": [("pe", 0)] * 6, "wdb": [("pe", 0)] * 2}

    def bwait(eng, ent):
        S.wait(eng, ent[0], ent[1])

    for dst, src in ((sgs, sg_d), (sus, su_d), (sdh, sd_d), (bqs, bq_d),
                     (bks, bk_d), (bvs, bv_d), (bos, bo_d), (cosq, cosq_d),
                     (sinq, sinq_d), (cosk, cosk_d), (sink, sink_d),
                     (idb, idb_d), (idf, idf_d), (onesm, onesm_d),
                     (epsv, epsv_d)):
        S.op("sync", "dma_start", out=dst.ap(), in_=src, inc=[("cst", 16)])
    consts_io = S.now("cst")
    S.wait("vector", "cst", consts_io)
    S.wait("tensor", "cst", consts_io)
    S.wait("scalar", "cst", consts_io)
    S.op("vector", "tensor_scalar", out=sdh.ap(), in0=sdh.ap(), scalar1=0.5,
         scalar2=None, op0=ALU.mult, inc=[("dve", 1)])

    ev_grp = {}
    ev_tb_pe = {}
    input_evts = {}
    g = 0
    for tb in range(TPC // 128):
        buf = tb % 2
        if tb >= 2:
            S.wait("sync", "pe", ev_tb_pe[tb - 2])
        S.op("sync", "dma_start", out=tokv[:, buf * D:(buf + 1) * D],
             in_=x_in[tb * 128:(tb + 1) * 128, :], inc=[("ip%d" % buf, 16)])
        io_here = S.now("ip%d" % buf)
        for ccg in range(CC // 4):
            par = g % 2
            w = []
            if ccg == 0:
                w.append(("ip%d" % buf, io_here))
            if g >= 2:
                w.append(("act", ev_grp[g - 2]))
            for j in range(4):
                cc = ccg * 4 + j
                S.op("tensor", "transpose",
                     paa[:, par * TTS + j * 128: par * TTS + (j + 1) * 128],
                     tokv[:, buf * D + cc * 128: buf * D + (cc + 1) * 128],
                     idf.ap(),
                     wait=w if j == 0 else (),
                     inc=[("pe", 1)] if j == 3 else [])
            S.op("scalar", "copy",
                 out=xr[:, ccg * 4:(ccg + 1) * 4, tb * 128:(tb + 1) * 128],
                 in_=paa[:, par * TTS:(par + 1) * TTS].rearrange("p (j t) -> p j t", j=4),
                 wait=[("pe", S.now("pe"))], inc=[("act", 1)])
            ev_grp[g] = S.now("act")
            g += 1
        ev_tb_pe[tb] = S.now("pe")
        input_evts[tb] = (S.now("act"), S.now("pe"))

    def ln_stats(tt, ps, cast_wait=(), stats_wait=(), dst=None, r_in_sq2=False):
        dv = xl if dst is None else dst
        tsl = slice(tt * TTS, (tt + 1) * TTS)
        cast_evt = {}
        for cc in range(CC):
            S.op("vector", "tensor_copy", out=dv[:, cc, :], in_=xr[:, cc, tsl],
                 wait=cast_wait if cc == 0 else (), inc=[("dve", 1)])
            cast_evt[cc] = S.now("dve")
        for cc in range(CC):
            w = list(stats_wait) + [("dve", cast_evt[CC - 1])] if cc == 0 else ()
            S.op("tensor", "matmul", ps[:, 0:TTS], lhsT=onesm.ap(),
                 rhs=dv[:, cc, :], start=(cc == 0), stop=(cc == CC - 1), wait=w)
        sq_evt = {}
        sqmm_evt = {}
        for cc in range(2):
            S.op("vector", "tensor_tensor", out=sq2[:, cc, :], in0=dv[:, cc, :],
                 in1=dv[:, cc, :], op=ALU.mult,
                 wait=[("dve", cast_evt[cc])], inc=[("dve", 1)])
            sq_evt[cc] = S.now("dve")
        for cc in range(CC):
            S.op("tensor", "matmul", ps[:, TTS:2 * TTS], lhsT=onesm.ap(),
                 rhs=sq2[:, cc % 2, :], start=(cc == 0), stop=(cc == CC - 1),
                 wait=[("dve", sq_evt[cc])], inc=[("pe", 1)])
            sqmm_evt[cc] = S.now("pe")
            if cc + 2 < CC:
                S.op("vector", "tensor_tensor", out=sq2[:, cc % 2, :],
                     in0=dv[:, cc + 2, :], in1=dv[:, cc + 2, :], op=ALU.mult,
                     wait=[("pe", sqmm_evt[cc]), ("dve", cast_evt[cc + 2])],
                     inc=[("dve", 1)])
                sq_evt[cc + 2] = S.now("dve")
        var_dst = vstage.ap()
        stats_pe = S.now("pe")
        S.op("scalar", "mul", sq2[:, 0, :], ps[:, 0:TTS], 1.0 / D,
             wait=[("pe", stats_pe)], inc=[("act", 1)])
        S.wait("vector", "act", S.now("act"))
        S.op("vector", "tensor_tensor", out=var_dst, in0=sq2[:, 0, :],
             in1=sq2[:, 0, :], op=ALU.mult)
        S.op("vector", "scalar_tensor_tensor", out=var_dst,
             in0=ps[:, TTS:2 * TTS], scalar=1.0 / D, in1=var_dst,
             op0=ALU.mult, op1=ALU.subtract, inc=[("dve", 1)])
        S.wait("scalar", "dve", S.now("dve"))
        S.op("scalar", "activation", out=ps[:, 0:TTS], in_=var_dst,
             func=AF.Ln, bias=epsv.ap(), scale=1.0)
        S.op("scalar", "activation", out=(sq2[:, 1, :] if r_in_sq2 else scr.ap()),
             in_=ps[:, 0:TTS], func=AF.Exp, scale=-0.5, inc=[("act", 1)])
        return S.now("act")

    def ln_apply(tt, r_act, dst=None, r_src=None):
        dv = xl if dst is None else dst
        rv = scr.ap() if r_src is None else r_src
        tsl = slice(tt * TTS, (tt + 1) * TTS)
        S.wait("vector", "act", r_act)
        for cc in range(CC):
            S.op("vector", "tensor_tensor", out=dv[:, cc, :],
                 in0=xr[:, cc, tsl], in1=sq2[:, 0, :], op=ALU.subtract)
            S.op("vector", "tensor_tensor", out=dv[:, cc, :], in0=dv[:, cc, :],
                 in1=rv, op=ALU.mult,
                 inc=[("dve", 1)] if cc == CC - 1 else [])

    def layernorm(tt):
        S.barrier()
        r_act = ln_stats(tt, paa)
        ln_apply(tt, r_act)

    def ffn(tt, ln_next=None, xv=None):
        S.barrier()
        if xv is None:
            xv = xl
        tsl = slice(tt * TTS, (tt + 1) * TTS)
        xln_ready = S.now("dve")
        ev_mm = {}
        ev_gevac = {}
        ev_fm = {}
        wd_g = {}
        wd_u = {}

        def issue_w(ic):
            pr = ic % 3
            bwait("gpsimd", bufstate["ws"][2 * pr])
            bwait("gpsimd", bufstate["ws"][2 * pr + 1])
            S.op("gpsimd", "dma_start", out=ws[:, 2 * pr, :], in_=wg_l[ic],
                 inc=[("w%d" % (2 * pr), 16)])
            wd_g[ic] = ("w%d" % (2 * pr), S.now("w%d" % (2 * pr)))
            S.op("gpsimd", "dma_start", out=ws[:, 2 * pr + 1, :], in_=wu_l[ic],
                 inc=[("w%d" % (2 * pr + 1), 16)])
            wd_u[ic] = ("w%d" % (2 * pr + 1), S.now("w%d" % (2 * pr + 1)))

        issue_w(0)
        issue_w(1)
        issue_w(2)
        for ic in range(NIC):
            bsel = ic % 2
            pr = ic % 3
            for cc in range(CC):
                w = []
                if cc == 0:
                    w = [wd_g[ic], ("dve", xln_ready)]
                    if ic >= 2:
                        w.append(("act", ev_gevac[ic - 2]))
                S.op("tensor", "matmul", paa[:, bsel * TTS:(bsel + 1) * TTS],
                     lhsT=ws[:, 2 * pr, cc * 128:(cc + 1) * 128],
                     rhs=xv[:, cc, :], start=(cc == 0), stop=(cc == CC - 1),
                     wait=w, inc=[("pe", 1)] if cc == CC - 1 else [])
            gate_pe = S.now("pe")
            for cc in range(CC):
                S.op("tensor", "matmul", pba[:, bsel * TTS:(bsel + 1) * TTS],
                     lhsT=ws[:, 2 * pr + 1, cc * 128:(cc + 1) * 128],
                     rhs=xv[:, cc, :], start=(cc == 0), stop=(cc == CC - 1),
                     wait=[wd_u[ic]] if cc == 0 else (),
                     inc=[("pe", 1)] if cc == CC - 1 else [])
            ev_mm[ic] = S.now("pe")
            bufstate["ws"][2 * pr] = bufstate["ws"][2 * pr + 1] = ("pe", ev_mm[ic])
            if ic + 3 < NIC:
                issue_w(ic + 3)
            g_sb = gub.ap()[:, bsel * TTS:(bsel + 1) * TTS]
            u_sb = gub.ap()[:, (2 + bsel) * TTS:(3 + bsel) * TTS]
            w = [("pe", gate_pe)]
            if ic >= 2:
                w.append(("dve", ev_fm[ic - 2]))
            S.op("scalar", "activation", out=g_sb,
                 in_=paa[:, bsel * TTS:(bsel + 1) * TTS], func=AF.Silu,
                 scale=sgs.ap()[:, ic:ic + 1], wait=w, inc=[("act", 1)])
            S.op("scalar", "activation", out=u_sb,
                 in_=pba[:, bsel * TTS:(bsel + 1) * TTS], func=AF.Copy,
                 scale=sus.ap()[:, ic:ic + 1], wait=[("pe", ev_mm[ic])],
                 inc=[("act", 1)])
            ev_gevac[ic] = S.now("act")
            S.op("vector", "tensor_tensor", out=fu[:, ic, :], in0=g_sb,
                 in1=u_sb, op=ALU.mult, wait=[("act", ev_gevac[ic])],
                 inc=[("dve", 1)])
            ev_fm[ic] = S.now("dve")
        fuse_done = S.now("dve")
        ev_dmm = {}
        ev_bank = {}
        ev_add = {}
        for q in range(4):
            for io in range(8):
                bsel = io % 2
                bwait("gpsimd", bufstate["wdb"][bsel])
                S.op("gpsimd", "dma_start", out=wdbv[:, bsel, :, :],
                     in_=wd_l[q, io].rearrange("p (k o) -> p k o", k=8),
                     inc=[("wd%d" % bsel, 16)])
                wready = ("wd%d" % bsel, S.now("wd%d" % bsel))
                for k in range(8):
                    for o in range(4):
                        w = []
                        if k == 0 and o == 0:
                            w = [wready]
                            if io == 0:
                                w.append(("dve", fuse_done))
                        if k == 0 and io == 0 and q > 0:
                            w.append(("act", ev_bank[(q - 1, o)]))
                        S.op("tensor", "matmul", pdna[:, o * TTS:(o + 1) * TTS],
                             lhsT=wdbv[:, bsel, k, o * 128:(o + 1) * 128],
                             rhs=fu[:, io * 8 + k, :],
                             start=(io == 0 and k == 0), stop=(io == 7 and k == 7),
                             wait=w,
                             inc=[("pe", 1)] if (k == 7 and o == 3) else [])
                ev_dmm[(q, io)] = S.now("pe")
                bufstate["wdb"][bsel] = ("pe", ev_dmm[(q, io)])
                if q == 0 and io == 0 and ln_next is not None:
                    r_act = ln_stats(ln_next, paa,
                                     cast_wait=[("pe", ev_mm[NIC - 1])],
                                     stats_wait=[("act", ev_gevac[NIC - 1])])
                    ln_apply(ln_next, r_act)
            qmm_pe = S.now("pe")
            for o in range(4):
                oc = q * 4 + o
                gi = q * 4 + o
                buf = dtmp if gi % 2 == 0 else rtmp
                w = [("pe", qmm_pe)]
                if (q, o - 2) in ev_add:
                    w.append(("dve", ev_add[(q, o - 2)]))
                elif q > 0 and o < 2:
                    w.append(("dve", ev_add[(q - 1, o + 2)]))
                S.op("scalar", "activation", out=buf.ap(),
                     in_=pdna[:, o * TTS:(o + 1) * TTS], func=AF.Copy,
                     scale=sdh.ap()[:, oc:oc + 1], wait=w, inc=[("act", 1)])
                ev_bank[(q, o)] = S.now("act")
                S.op("vector", "tensor_tensor", out=xr[:, oc, tsl],
                     in0=xr[:, oc, tsl], in1=buf.ap(), op=ALU.add,
                     wait=[("act", ev_bank[(q, o)])], inc=[("dve", 1)])
                ev_add[(q, o)] = S.now("dve")

    qkv_state = {"i": 0, "vtr_dve": 0, "ev_evac": [0, 0], "xln_ready": 0}

    def qkv(tt, kinds, xv=None, bar=True):
        if bar:
            S.barrier()
        if xv is None:
            xv = xl
        xln_ready = S.now("dve")
        ev_slot = {}
        ev_evac = {}
        state = qkv_state
        state["xln_ready"] = xln_ready

        def proj(kind, n_oc, w_dram, bias_sb):
            xln_ready = state["xln_ready"]
            for oc in range(n_oc):
                slot = state["i"] % 5
                par = state["i"] % 2
                bwait("gpsimd", bufstate["ws"][slot])
                S.op("gpsimd", "dma_start", out=ws[:, slot, :], in_=w_dram[oc],
                     inc=[("w%d" % slot, 16)])
                wready = ("w%d" % slot, S.now("w%d" % slot))
                for cc in range(CC):
                    w = []
                    if cc == 0:
                        w = [wready, ("dve", xln_ready)]
                        if state["ev_evac"][par]:
                            w.append(("act", state["ev_evac"][par]))
                    S.op("tensor", "matmul", paa[:, par * TTS:(par + 1) * TTS],
                         lhsT=ws[:, slot, cc * 128:(cc + 1) * 128],
                         rhs=xv[:, cc, :], start=(cc == 0), stop=(cc == CC - 1),
                         wait=w, inc=[("pe", 1)] if cc == CC - 1 else [])
                ev_slot[slot] = S.now("pe")
                bufstate["ws"][slot] = ("pe", ev_slot[slot])
                psrc = paa[:, par * TTS:(par + 1) * TTS]
                if kind == "v":
                    S.op("scalar", "activation", out=vstage.ap(), in_=psrc,
                         func=AF.Identity, bias=bias_sb[:, oc:oc + 1], scale=1.0,
                         wait=[("pe", ev_slot[slot]), ("dve", state["vtr_dve"])],
                         inc=[("act", 1)])
                    state["ev_evac"][par] = S.now("act")
                    for j in range(4):
                        w = []
                        if j == 0:
                            w = [("act", state["ev_evac"][par]), ("dve", state["vtr_dve"])]
                        S.op("tensor", "transpose",
                             pdnb[:, j * 128:(j + 1) * 128],
                             vstage.ap()[:, j * 128:(j + 1) * 128], idb.ap(),
                             wait=w, inc=[("pe", 1)] if j == 3 else [])
                    S.op("vector", "tensor_copy",
                         out=vov[:, tt * 4:(tt + 1) * 4, oc * 128:(oc + 1) * 128],
                         in_=pdnb[:, 0:TTS].rearrange("p (j t) -> p j t", j=4),
                         wait=[("pe", S.now("pe"))], inc=[("dve", 1)])
                    state["vtr_dve"] = S.now("dve")
                else:
                    S.op("scalar", "activation", out=rtmp.ap(), in_=psrc,
                         func=AF.Identity, bias=bias_sb[:, oc:oc + 1], scale=1.0,
                         wait=[("pe", ev_slot[slot]), ("dve", S.now("dve"))],
                         inc=[("act", 1)])
                    state["ev_evac"][par] = S.now("act")
                    cosv = (cosq if kind == "q" else cosk).ap()[:, oc:oc + 1]
                    sinv = (sinq if kind == "q" else sink).ap()[:, oc:oc + 1]
                    dest = (qpv[:, oc, tt * TTS:(tt + 1) * TTS] if kind == "q"
                            else kov[:, oc, tt * TTS:(tt + 1) * TTS])
                    S.wait("vector", "act", state["ev_evac"][par])
                    S.op("vector", "tensor_copy", out=scr.ap()[0:64, :],
                         in_=rtmp.ap()[64:128, :])
                    S.op("vector", "tensor_copy", out=scr.ap()[64:128, :],
                         in_=rtmp.ap()[0:64, :])
                    S.op("vector", "tensor_scalar", out=scr.ap(), in0=scr.ap(),
                         scalar1=sinv, scalar2=None, op0=ALU.mult)
                    S.op("vector", "scalar_tensor_tensor", out=dest,
                         in0=rtmp.ap(), scalar=cosv, in1=scr.ap(),
                         op0=ALU.mult, op1=ALU.add, inc=[("dve", 1)])
                state["i"] += 1

        for kind in kinds:
            if kind == "k":
                proj("k", KV, wk_l, bks.ap())
            elif kind == "v":
                proj("v", KV, wv_l, bvs.ap())
            else:
                proj("q", H, wq_l, bqs.ap())

    def exchange_send(tt):
        dve_done = S.now("dve")
        S.wait("sync", "dve", dve_done)
        ksem = "kv%d" % tt
        S.op("sync", "dma_start",
             out=cc_in[tt].ap()[0:KV * 128, :].rearrange("(kv p) t -> p kv t", p=128),
             in_=kov[:, :, tt * TTS:(tt + 1) * TTS], inc=[(ksem, 16)])
        S.op("sync", "dma_start",
             out=cc_in[tt].ap()[KV * 128:2 * KV * 128, :].rearrange("(s p) d -> p s d", p=128),
             in_=vov[:, tt * 4:(tt + 1) * 4, :], inc=[(ksem, 16)])

    def exchange_trigger(tt):
        ksem = "kv%d" % tt
        S.wait("gpsimd", ksem, S.now(ksem))
        S.op("gpsimd", "collective_compute", "AllGather", ALU.bypass,
             replica_groups=[[0, 1], [2, 3], [4, 5], [6, 7]],
             ins=[cc_in[tt].ap().opt()], outs=[cc_out[tt].ap().opt()],
             inc=[("cc", 1)])

    def exchange_recv():
        ios = []
        for tt in range(2):
            q = "sync" if tt == 0 else "gpsimd"
            rsem = "rc%d" % tt
            S.wait(q, "cc", tt + 1)
            for r in range(2):
                base = r * 2 * KV * 128
                S.op(q, "dma_start",
                     out=kfv[:, :, r * TPC + tt * TTS: r * TPC + (tt + 1) * TTS],
                     in_=cc_out[tt].ap()[base:base + KV * 128, :]
                         .rearrange("(kv p) t -> p kv t", p=128),
                     inc=[(rsem, 16)])
                S.op(q, "dma_start",
                     out=vtv[:, r * 8 + tt * 4: r * 8 + (tt + 1) * 4, :],
                     in_=cc_out[tt].ap()[base + KV * 128:base + 2 * KV * 128, :]
                         .rearrange("(s p) d -> p s d", p=128),
                     inc=[(rsem, 16)])
            ios.append((rsem, S.now(rsem)))
        return ios

    def load_mask(tt, free_pe):
        S.wait("sync", "pe", free_pe)
        S.op("sync", "dma_start",
             out=masksb,
             in_=mask_d[:, tt * TTS:(tt + 1) * TTS].rearrange("(s p) t -> p s t", p=128),
             inc=[("msk", 16)])
        return S.now("msk")

    SIG = (0, 1, 2, 3, 8, 9, 10, 11, 4, 5, 6, 7, 12, 13, 14, 15)

    def scores(tt, kv_io0, kv_io1, mask_io):
        S.barrier()
        pslot = [paa[:, 0:TTS], paa[:, TTS:2 * TTS],
                 pdna[:, 2 * TTS:3 * TTS], pdna[:, 3 * TTS:4 * TTS]]
        ppair = [paa[:, 0:2 * TTS], pdna[:, 2 * TTS:4 * TTS]]
        exp_evt = {}
        avz_evt = {}
        out_dve = [0, 0]
        z_act = [0, 0]

        def emit_avz(h, j, first_banks_w=()):
            kv = h // GRP
            s = SIG[j]
            pz = pba if h % 2 == 0 else pdna
            w = [("act", exp_evt[(h, j)])]
            if j == 0:
                w += list(first_banks_w)
            S.op("tensor", "matmul", pz[:, 0:TTS],
                 lhsT=vtv[:, s, kv * 128:(kv + 1) * 128], rhs=eTv[:, j % 4, :],
                 start=(j == 0), stop=(j == 15), wait=w)
            S.op("tensor", "matmul", pz[:, TTS:2 * TTS],
                 lhsT=onesm.ap(), rhs=eTv[:, j % 4, :],
                 start=(j == 0), stop=(j == 15), inc=[("pe", 1)])
            avz_evt[(h, j)] = S.now("pe")

        def emit_norm(h):
            pz = pba if h % 2 == 0 else pdna
            zs = dtmp if h % 2 == 0 else rtmp
            S.op("scalar", "activation", out=scr.ap(), in_=pz[:, TTS:2 * TTS],
                 func=AF.Ln, wait=[("pe", avz_evt[(h, 15)])])
            S.op("scalar", "activation", out=zs.ap(), in_=scr.ap(), func=AF.Exp,
                 scale=-1.0, inc=[("act", 1)])
            z_act[h % 2] = S.now("act")
            S.op("vector", "tensor_tensor",
                 out=ofv[:, h, :],
                 in0=pz[:, 0:TTS], in1=zs.ap(), op=ALU.mult,
                 wait=[("act", S.now("act"))], inc=[("dve", 1)])
            out_dve[h % 2] = S.now("dve")

        for h in range(H):
            kv = h // GRP
            qtile = qpv[:, h, tt * TTS:(tt + 1) * TTS]
            for j in range(16):
                s = SIG[j]
                w = []
                if h == 0 and j == 0:
                    w = [kv_io0, ("msk", mask_io)]
                if h == 0 and j == 8:
                    w = [kv_io1]
                key = (h, j - 4) if j >= 4 else ((h - 1, 12 + j) if h > 0 else None)
                if key is not None:
                    w.append(("act", exp_evt[key]))
                S.op("tensor", "matmul", pslot[j % 4],
                     lhsT=kfv[:, kv, s * 128:(s + 1) * 128], rhs=qtile,
                     start=True, stop=False, wait=w)
                S.op("tensor", "matmul", pslot[j % 4],
                     lhsT=idb.ap(), rhs=masksb[:, s, :],
                     start=False, stop=True, inc=[("pe", 1)])
                score_pe = S.now("pe")
                key = (h, j - 4) if j >= 4 else ((h - 1, 12 + j) if h > 0 else None)
                w = [("pe", score_pe)]
                if key is not None:
                    w.append(("pe", avz_evt[key]))
                S.op("scalar", "activation", out=eTv[:, j % 4, :],
                     in_=pslot[j % 4], func=AF.Exp, scale=1.0,
                     wait=w, inc=[("act", 1)])
                exp_evt[(h, j)] = S.now("act")
                if j >= 3:
                    fb = ()
                    if j == 3:
                        fb = [("dve", out_dve[h % 2]), ("act", z_act[h % 2])]
                    emit_avz(h, j - 3, fb)
            emit_avz(h, 13)
            emit_avz(h, 14)
            emit_avz(h, 15)
            emit_norm(h)
        bufstate["ws"][4] = ("pe", S.now("pe"))
        return S.now("pe")

    def oproj(tt, ln_next=None):
        S.barrier()
        of_ready = S.now("dve")
        ev_slot = {}
        ev_evac = {}
        for oc in range(16):
            if oc == 8 and ln_next is not None:
                r_act = ln_stats(ln_next, pba)
                ln_apply(ln_next, r_act)
            slot = oc % 4
            par = oc % 2
            bwait("gpsimd", bufstate["ws"][slot])
            S.op("gpsimd", "dma_start", out=ws[:, slot, :], in_=wo_l[oc],
                 inc=[("w%d" % slot, 16)])
            wready = ("w%d" % slot, S.now("w%d" % slot))
            for cc in range(CC):
                w = []
                if cc == 0:
                    w = [wready, ("dve", of_ready)]
                    if oc >= 2:
                        w.append(("act", ev_evac[par]))
                S.op("tensor", "matmul", paa[:, par * TTS:(par + 1) * TTS],
                     lhsT=ws[:, slot, cc * 128:(cc + 1) * 128],
                     rhs=ofv[:, cc, :],
                     start=(cc == 0), stop=(cc == CC - 1), wait=w,
                     inc=[("pe", 1)] if cc == CC - 1 else [])
            ev_slot[slot] = S.now("pe")
            bufstate["ws"][slot] = ("pe", ev_slot[slot])
            if oc == 15:
                bufstate["ws"][5] = ("pe", ev_slot[slot])
                bufstate["wdb"][0] = ("pe", ev_slot[slot])
                bufstate["wdb"][1] = ("pe", ev_slot[slot])
            S.op("scalar", "activation", out=dtmp.ap(),
                 in_=paa[:, par * TTS:(par + 1) * TTS], func=AF.Identity,
                 bias=bos.ap()[:, oc:oc + 1], scale=1.0,
                 wait=[("pe", ev_slot[slot]), ("dve", S.now("dve"))],
                 inc=[("act", 1)])
            ev_evac[par] = S.now("act")
            S.op("vector", "tensor_tensor",
                 out=xr[:, oc, tt * TTS:(tt + 1) * TTS],
                 in0=xr[:, oc, tt * TTS:(tt + 1) * TTS], in1=dtmp.ap(),
                 op=ALU.add, wait=[("act", ev_evac[par])], inc=[("dve", 1)])

    have1, haveA, have2 = ("ffn1" in parts), ("attn" in parts), ("ffn2" in parts)
    EMB = EMBED_LN
    if parts:
        r0 = ln_stats(0, pba, cast_wait=[("act", input_evts[3][0])],
                      stats_wait=[("pe", input_evts[3][1])])
        ln_apply(0, r0)
    if have1:
        ffn(0, ln_next=1 if EMB else None)
        if not EMB:
            layernorm(1)
        ffn(1, ln_next=(0 if (haveA or have2) else None) if EMB else None)
    if haveA:
        if have1 and not EMB:
            layernorm(0)
        qkv(0, ("k", "v"))
        exchange_send(0)
        r21 = ln_stats(1, pba, dst=xl2, r_in_sq2=True)
        ln_apply(1, r21, dst=xl2, r_src=sq2[:, 1, :])
        qkv(0, ("q",), bar=False)
        exchange_trigger(0)
        q0_pe = S.now("pe")
        qkv(1, ("k", "v"), xv=xl2, bar=False)
        exchange_send(1)
        exchange_trigger(1)
        qkv(1, ("q",), xv=xl2, bar=False)
        mask_io0 = load_mask(0, q0_pe)
        kv_io0, kv_io1 = exchange_recv()
        mfree = scores(0, kv_io0, kv_io1, mask_io0)
        oproj(0)
        mask_io1 = load_mask(1, mfree)
        scores(1, kv_io0, kv_io1, mask_io1)
        oproj(1, ln_next=0 if (have2 and EMB) else None)
    if have2:
        if (have1 or haveA) and not EMB:
            layernorm(0)
        ffn(0, ln_next=1 if EMB else None)
        if not EMB:
            layernorm(1)
        ffn(1, ln_next=None)

    S.barrier()
    ev_grp = {}
    ev_dma = {}
    g = 0
    first = True
    for tb in range(TPC // 128):
        buf = tb % 2
        for ccg in range(CC // 4):
            par = g % 2
            w = []
            if first:
                w = [("dve", S.now("dve")), ("act", S.now("act"))]
                first = False
            if g >= 2:
                w.append(("act", ev_grp[g - 2]))
            for j in range(4):
                cc = ccg * 4 + j
                S.op("tensor", "transpose",
                     paa[:, par * TTS + j * 128: par * TTS + (j + 1) * 128],
                     xr[:, cc, tb * 128:(tb + 1) * 128], idf.ap(),
                     wait=w if j == 0 else (),
                     inc=[("pe", 1)] if j == 3 else [])
            wloc = [("pe", S.now("pe"))]
            if ccg == 0 and tb >= 2:
                wloc.append(("ip%d" % buf, ev_dma[tb - 2]))
            S.op("scalar", "copy",
                 out=tokv[:, buf * D + ccg * 512: buf * D + (ccg + 1) * 512],
                 in_=paa[:, par * TTS:(par + 1) * TTS],
                 wait=wloc, inc=[("act", 1)])
            ev_grp[g] = S.now("act")
            g += 1
        S.op("sync", "dma_start", out=y_out[tb * 128:(tb + 1) * 128, :],
             in_=tokv[:, buf * D:(buf + 1) * D],
             wait=[("act", S.now("act"))], inc=[("ip%d" % buf, 16)])
        ev_dma[tb] = S.now("ip%d" % buf)
    S.wait("sync", "ip0", S.now("ip0"))
    S.wait("sync", "ip1", S.now("ip1"))

    with nc.Block() as block:
        @block.tensor
        def _(e):
            for fn in S.q["tensor"]:
                fn(e, sems)

        @block.scalar
        def _(e):
            for fn in S.q["scalar"]:
                fn(e, sems)

        @block.vector
        def _(e):
            for fn in S.q["vector"]:
                fn(e, sems)

        @block.sync
        def _(e):
            for fn in S.q["sync"]:
                fn(e, sems)

        @block.gpsimd
        def _(e):
            for fn in S.q["gpsimd"]:
                fn(e, sems)

    return nc


def _host_prep(inputs):
    gi = lambda k: np.ascontiguousarray(np.asarray(inputs[k]))
    x = gi("L0_attn_x")
    wq = gi("L0_attn_wq"); bq = gi("L0_attn_bq")
    wk = gi("L0_attn_wk"); bk = gi("L0_attn_bk")
    wv = gi("L0_attn_wv"); bv = gi("L0_attn_bv")
    wo = gi("L0_attn_wo"); bo = gi("L0_attn_bo")
    wg = gi("L0_ffn_w_gate"); wu = gi("L0_ffn_w_up"); wd = gi("L0_ffn_w_down")
    sgv = gi("L0_ffn_s_gate"); suv = gi("L0_ffn_s_up"); sdv = gi("L0_ffn_s_down")

    def tile_oc(w, noc):
        return np.ascontiguousarray(
            w.reshape(noc, 128, CC, 128).transpose(0, 3, 2, 1).reshape(noc, 128, D))

    wg_l = np.ascontiguousarray(
        wg.reshape(CC, 128, NIC, 128).transpose(2, 1, 0, 3).reshape(NIC, 128, D))
    wu_l = np.ascontiguousarray(
        wu.reshape(CC, 128, NIC, 128).transpose(2, 1, 0, 3).reshape(NIC, 128, D))
    wd_l = np.ascontiguousarray(
        wd.reshape(8, 8, 128, 4, 512).transpose(3, 0, 2, 1, 4).reshape(4, 8, 128, 8 * 512))

    pp = lambda v, n: np.ascontiguousarray(v.reshape(n, 128).T.astype(np.float32))
    consts = {
        "wq": tile_oc(wq, 16), "wk": tile_oc(wk, 4), "wv": tile_oc(wv, 4),
        "wo": tile_oc(wo, 16), "wg": wg_l, "wu": wu_l, "wdn": wd_l,
        "sg": pp(sgv, NIC), "su": pp(suv, NIC), "sd": pp(sdv, CC),
        "bq": pp(bq, 16), "bk": pp(bk, 4), "bv": pp(bv, 4), "bo": pp(bo, 16),
    }
    j = np.arange(64, dtype=np.float64)
    inv = 1.0 / (10000.0 ** (2.0 * j / 128.0))
    scl = 1.0 / np.sqrt(128.0)
    for name, nh, s in (("q", 16, scl), ("k", 4, 1.0)):
        hh = np.arange(nh, dtype=np.float64)
        ang = inv[:, None] * hh[None, :]
        consts["cos" + name] = np.ascontiguousarray(
            (np.concatenate([np.cos(ang), np.cos(ang)], 0) * s).astype(np.float32))
        consts["sin" + name] = np.ascontiguousarray(
            (np.concatenate([-np.sin(ang), np.sin(ang)], 0) * s).astype(np.float32))
    consts["idb"] = np.eye(128, dtype=ml_dtypes.bfloat16)
    consts["idf"] = np.eye(128, dtype=np.float32)
    consts["onesm"] = np.ones((128, 128), dtype=ml_dtypes.bfloat16)
    consts["epsv"] = np.full((128, 1), EPS, dtype=np.float32)

    maskT = np.tril(np.ones((T, T), dtype=np.float32)).T.astype(ml_dtypes.bfloat16)
    in_maps = []
    for c in range(8):
        b, half = c // 2, c % 2
        m = dict(consts)
        m["x"] = np.ascontiguousarray(x[b, half * TPC:(half + 1) * TPC, :])
        m["mask"] = np.ascontiguousarray(maskT[:, half * TPC:(half + 1) * TPC])
        in_maps.append(m)
    return in_maps


def kernel(**inputs):
    nc = _build(ALL_PARTS)
    in_maps = _host_prep(inputs)
    res = run_bass_kernel_spmd(nc, in_maps, core_ids=list(range(8)))
    out = np.empty((B, T, D), np.float32)
    for c in range(8):
        b, half = c // 2, c % 2
        out[b, half * TPC:(half + 1) * TPC, :] = res.results[c]["y"]
    return out


# version 22 (final)
# speedup vs baseline: 1.1345x; 1.0096x over previous
import numpy as np
import ml_dtypes
import concourse.bass as bass
import concourse.mybir as mybir
from concourse.bass_utils import run_bass_kernel_spmd

F32 = mybir.dt.float32
BF16 = mybir.dt.bfloat16
I8 = mybir.dt.int8
AF = mybir.ActivationFunctionType
ALU = mybir.AluOpType

B, T, D = 4, 2048, 2048
H, KV, HD = 16, 4, 128
GRP = H // KV
INTER = 8192
TPC = 1024
TTS = 512
NTT = TPC // TTS
CC = D // 128
NIC = INTER // 128
EPS = 1e-5
KB = 1024

ALL_PARTS = ("ffn1", "attn", "ffn2")
EMBED_LN = True


class Sched:

    def __init__(self):
        self.q = {e: [] for e in ("tensor", "scalar", "vector", "sync", "gpsimd")}
        self.tot = {s: 0 for s in
                    ("pe", "act", "dve", "cc", "cst",
                     "w0", "w1", "w2", "w3", "w4", "w5",
                     "wd0", "wd1", "msk", "kv0", "kv1", "rc0", "rc1",
                     "ip0", "ip1", "pcw")}

    def now(self, sem):
        return self.tot[sem]

    def op(self, eng, fname, *a, wait=(), inc=(), **k):
        wait = [w for w in wait if w is not None and w[1] > 0]
        for s, v in wait:
            assert v <= self.tot[s], (eng, fname, s, v, self.tot[s])
        for s, n in inc:
            self.tot[s] += n
        wait = tuple(wait)
        inc = tuple(inc)

        def fn(e, sems, fname=fname, a=a, k=k, wait=wait, inc=inc):
            for s, v in wait:
                e.wait_ge(sems[s], v)
            ins = getattr(e, fname)(*a, **k)
            for s, n in inc:
                ins = ins.then_inc(sems[s], n)

        self.q[eng].append(fn)

    def wait(self, eng, sem, val):
        assert val <= self.tot[sem], (eng, sem, val, self.tot[sem])
        if val <= 0:
            return

        def fn(e, sems, sem=sem, val=val):
            e.wait_ge(sems[sem], val)

        self.q[eng].append(fn)

    def barrier(self):
        snap = {s: self.tot[s] for s in ("pe", "act", "dve")}
        own = {"tensor": "pe", "scalar": "act", "vector": "dve"}
        for e in ("tensor", "scalar", "vector"):
            for s, v in snap.items():
                if s != own[e]:
                    self.wait(e, s, v)


def _build(parts=ALL_PARTS):
    nc = bass.Bass("TRN2", target_bir_lowering=False, debug=False)
    nc.detect_race_conditions = False
    S = Sched()

    x_in = nc.dram_tensor("x", [TPC, D], F32, kind="ExternalInput").ap()
    y_out = nc.dram_tensor("y", [TPC, D], F32, kind="ExternalOutput").ap()
    wg_l = nc.dram_tensor("wg", [NIC, 128, D], I8, kind="ExternalInput").ap()
    wu_l = nc.dram_tensor("wu", [NIC, 128, D], I8, kind="ExternalInput").ap()
    wd_l = nc.dram_tensor("wdn", [4, 8, 128, 8 * 512], I8, kind="ExternalInput").ap()
    wq_l = nc.dram_tensor("wq", [16, 128, D], F32, kind="ExternalInput").ap()
    wk_l = nc.dram_tensor("wk", [4, 128, D], F32, kind="ExternalInput").ap()
    wv_l = nc.dram_tensor("wv", [4, 128, D], F32, kind="ExternalInput").ap()
    wo_l = nc.dram_tensor("wo", [16, 128, D], F32, kind="ExternalInput").ap()
    sg_d = nc.dram_tensor("sg", [128, NIC], F32, kind="ExternalInput").ap()
    su_d = nc.dram_tensor("su", [128, NIC], F32, kind="ExternalInput").ap()
    sd_d = nc.dram_tensor("sd", [128, CC], F32, kind="ExternalInput").ap()
    bq_d = nc.dram_tensor("bq", [128, 16], F32, kind="ExternalInput").ap()
    bk_d = nc.dram_tensor("bk", [128, 4], F32, kind="ExternalInput").ap()
    bv_d = nc.dram_tensor("bv", [128, 4], F32, kind="ExternalInput").ap()
    bo_d = nc.dram_tensor("bo", [128, 16], F32, kind="ExternalInput").ap()
    cosq_d = nc.dram_tensor("cosq", [128, 16], F32, kind="ExternalInput").ap()
    sinq_d = nc.dram_tensor("sinq", [128, 16], F32, kind="ExternalInput").ap()
    cosk_d = nc.dram_tensor("cosk", [128, 4], F32, kind="ExternalInput").ap()
    sink_d = nc.dram_tensor("sink", [128, 4], F32, kind="ExternalInput").ap()
    mask_d = nc.dram_tensor("mask", [T, TPC], BF16, kind="ExternalInput").ap()
    idb_d = nc.dram_tensor("idb", [128, 128], BF16, kind="ExternalInput").ap()
    idf_d = nc.dram_tensor("idf", [128, 128], F32, kind="ExternalInput").ap()
    onesm_d = nc.dram_tensor("onesm", [128, 128], BF16, kind="ExternalInput").ap()
    epsv_d = nc.dram_tensor("epsv", [128, 1], F32, kind="ExternalInput").ap()
    mb_d = nc.dram_tensor("mbias", [128, 32], F32, kind="ExternalInput").ap()
    wqb_l = nc.dram_tensor("wqb", [16, 128, D], BF16).ap()
    wkb_l = nc.dram_tensor("wkb", [4, 128, D], BF16).ap()
    wvb_l = nc.dram_tensor("wvb", [4, 128, D], BF16).ap()
    wob_l = nc.dram_tensor("wob", [16, 128, D], BF16).ap()
    cc_in = [nc.dram_tensor(f"cc_in{t}", [2 * KV * 128, TTS], BF16) for t in range(2)]
    cc_out = [nc.dram_tensor(f"cc_out{t}", [4 * KV * 128, TTS], BF16) for t in range(2)]

    x_res = nc.alloc_sbuf_tensor("x_res", [128, CC * TPC], F32)
    xln = nc.alloc_sbuf_tensor("xln", [128, CC * TTS], BF16)
    scr = nc.alloc_sbuf_tensor("scr", [128, TTS], F32)
    dtmp = nc.alloc_sbuf_tensor("dtmp", [128, TTS], F32)
    rtmp = nc.alloc_sbuf_tensor("rtmp", [128, TTS], F32)
    sgs = nc.alloc_sbuf_tensor("sgs", [128, NIC], F32)
    sus = nc.alloc_sbuf_tensor("sus", [128, NIC], F32)
    sdh = nc.alloc_sbuf_tensor("sdh", [128, CC], F32)
    bqs = nc.alloc_sbuf_tensor("bqs", [128, 16], F32)
    bks = nc.alloc_sbuf_tensor("bks", [128, 4], F32)
    bvs = nc.alloc_sbuf_tensor("bvs", [128, 4], F32)
    bos = nc.alloc_sbuf_tensor("bos", [128, 16], F32)
    cosq = nc.alloc_sbuf_tensor("cosq_s", [128, 16], F32)
    sinq = nc.alloc_sbuf_tensor("sinq_s", [128, 16], F32)
    cosk = nc.alloc_sbuf_tensor("cosk_s", [128, 4], F32)
    sink = nc.alloc_sbuf_tensor("sink_s", [128, 4], F32)
    idb = nc.alloc_sbuf_tensor("idb_s", [128, 128], BF16)
    idf = nc.alloc_sbuf_tensor("idf_s", [128, 128], F32)
    onesm = nc.alloc_sbuf_tensor("onesm_s", [128, 128], BF16)
    epsv = nc.alloc_sbuf_tensor("epsv_s", [128, 1], F32)
    mbias = nc.alloc_sbuf_tensor("mbias_s", [128, 32], F32)

    RB = (nc.sbuf_base + 31) & ~31
    fuse = nc.alloc_sbuf_tensor_at("fuse", [128, NIC * TTS], BF16, offset=RB)
    wslot = nc.alloc_sbuf_tensor_at("wslot", [128, 6 * D], BF16, offset=RB + 64 * KB)
    wdb = nc.alloc_sbuf_tensor_at("wdb", [128, 2 * 8 * 512], BF16, offset=RB + 88 * KB)
    gub = nc.alloc_sbuf_tensor_at("gub", [128, 4 * TTS], BF16, offset=RB + 104 * KB)
    qp = nc.alloc_sbuf_tensor_at("qp", [128, H * TPC], BF16, offset=RB)
    kf = nc.alloc_sbuf_tensor_at("kf", [128, KV * 2 * TPC], BF16, offset=RB + 32 * KB)
    vt = nc.alloc_sbuf_tensor_at("vt", [128, 16 * TTS], BF16, offset=RB + 48 * KB)
    eT = nc.alloc_sbuf_tensor_at("eT", [128, 4 * TTS], BF16, offset=RB + 80 * KB)
    o_fm = nc.alloc_sbuf_tensor_at("o_fm", [128, H * TTS], BF16, offset=RB + 84 * KB)
    kown = nc.alloc_sbuf_tensor_at("kown", [128, KV * TPC], BF16, offset=RB + 84 * KB)
    vown = nc.alloc_sbuf_tensor_at("vown", [128, 8 * TTS], BF16, offset=RB + 92 * KB)
    xln2 = nc.alloc_sbuf_tensor_at("xln2", [128, CC * TTS], BF16, offset=RB + 100 * KB)
    vstage = nc.alloc_sbuf_tensor_at("vstage", [128, TTS], BF16, offset=RB + 116 * KB)
    sqscr = nc.alloc_sbuf_tensor_at("sqscr", [128, 2 * TTS], BF16, offset=RB + 117 * KB)
    tok = nc.alloc_sbuf_tensor_at("tok", [128, 2 * D], F32, offset=RB)
    REGION_END = RB + 119 * KB
    assert REGION_END <= 229344, (RB, REGION_END)

    pa = nc.alloc_psum_tensor("pa", [128, 2 * TTS], F32)
    pb = nc.alloc_psum_tensor("pb", [128, 2 * TTS], F32)
    pdn = nc.alloc_psum_tensor("pdn", [128, 4 * TTS], F32)

    sems = {s: nc.alloc_semaphore("s_" + s)
            for s in ("pe", "act", "dve", "cc", "cst",
                      "w0", "w1", "w2", "w3", "w4", "w5",
                      "wd0", "wd1", "msk", "kv0", "kv1", "rc0", "rc1",
                      "ip0", "ip1", "pcw")}

    xr = x_res.ap().rearrange("p (cc t) -> p cc t", cc=CC)
    xl = xln.ap().rearrange("p (cc t) -> p cc t", cc=CC)
    fu = fuse.ap().rearrange("p (ic t) -> p ic t", ic=NIC)
    ws = wslot.ap().rearrange("p (s o) -> p s o", s=6)
    wdbv = wdb.ap().rearrange("p (b k o) -> p b k o", b=2, k=8)
    qpv = qp.ap().rearrange("p (h t) -> p h t", h=H)
    kfv = kf.ap().rearrange("p (kv s) -> p kv s", kv=KV)
    vtv = vt.ap().rearrange("p (s d) -> p s d", s=16)
    ofv = o_fm.ap().rearrange("p (h t) -> p h t", h=H)
    xl2 = xln2.ap().rearrange("p (cc t) -> p cc t", cc=CC)
    kov = kown.ap().rearrange("p (kv t) -> p kv t", kv=KV)
    vov = vown.ap().rearrange("p (s d) -> p s d", s=8)
    eTv = eT.ap().rearrange("p (s t) -> p s t", s=4)
    tokv = tok.ap()
    masksb = xln.ap().rearrange("p (s t) -> p s t", s=CC)
    sq2 = sqscr.ap().rearrange("p (b t) -> p b t", b=2)
    paa, pba, pdna = pa.ap(), pb.ap(), pdn.ap()
    pdnb = pdn.bitcast(BF16).ap()
    bufstate = {"ws": [("pe", 0)] * 6, "wdb": [("pe", 0)] * 2}

    def bwait(eng, ent):
        S.wait(eng, ent[0], ent[1])

    for dst, src in ((sgs, sg_d), (sus, su_d), (sdh, sd_d), (bqs, bq_d),
                     (bks, bk_d), (bvs, bv_d), (bos, bo_d), (cosq, cosq_d),
                     (sinq, sinq_d), (cosk, cosk_d), (sink, sink_d),
                     (idb, idb_d), (idf, idf_d), (onesm, onesm_d),
                     (epsv, epsv_d), (mbias, mb_d)):
        S.op("sync", "dma_start", out=dst.ap(), in_=src, inc=[("cst", 16)])
    consts_io = S.now("cst")
    S.wait("vector", "cst", consts_io)
    S.wait("tensor", "cst", consts_io)
    S.wait("scalar", "cst", consts_io)
    S.op("vector", "tensor_scalar", out=sdh.ap(), in0=sdh.ap(), scalar1=0.5,
         scalar2=None, op0=ALU.mult, inc=[("dve", 1)])

    ev_grp = {}
    ev_tb_pe = {}
    input_evts = {}
    g = 0
    for tb in range(TPC // 128):
        buf = tb % 2
        if tb >= 2:
            S.wait("sync", "pe", ev_tb_pe[tb - 2])
        S.op("sync", "dma_start", out=tokv[:, buf * D:(buf + 1) * D],
             in_=x_in[tb * 128:(tb + 1) * 128, :], inc=[("ip%d" % buf, 16)])
        io_here = S.now("ip%d" % buf)
        for ccg in range(CC // 4):
            par = g % 2
            w = []
            if ccg == 0:
                w.append(("ip%d" % buf, io_here))
            if g >= 2:
                w.append(("act", ev_grp[g - 2]))
            for j in range(4):
                cc = ccg * 4 + j
                S.op("tensor", "transpose",
                     paa[:, par * TTS + j * 128: par * TTS + (j + 1) * 128],
                     tokv[:, buf * D + cc * 128: buf * D + (cc + 1) * 128],
                     idf.ap(),
                     wait=w if j == 0 else (),
                     inc=[("pe", 1)] if j == 3 else [])
            S.op("scalar", "copy",
                 out=xr[:, ccg * 4:(ccg + 1) * 4, tb * 128:(tb + 1) * 128],
                 in_=paa[:, par * TTS:(par + 1) * TTS].rearrange("p (j t) -> p j t", j=4),
                 wait=[("pe", S.now("pe"))], inc=[("act", 1)])
            ev_grp[g] = S.now("act")
            g += 1
        ev_tb_pe[tb] = S.now("pe")
        input_evts[tb] = (S.now("act"), S.now("pe"))

    def ln_stats(tt, ps, cast_wait=(), stats_wait=(), dst=None, r_in_sq2=False):
        dv = xl if dst is None else dst
        tsl = slice(tt * TTS, (tt + 1) * TTS)
        cast_evt = {}
        for cc in range(CC):
            S.op("vector", "tensor_copy", out=dv[:, cc, :], in_=xr[:, cc, tsl],
                 wait=cast_wait if cc == 0 else (), inc=[("dve", 1)])
            cast_evt[cc] = S.now("dve")
        for cc in range(CC):
            w = list(stats_wait) + [("dve", cast_evt[CC - 1])] if cc == 0 else ()
            S.op("tensor", "matmul", ps[:, 0:TTS], lhsT=onesm.ap(),
                 rhs=dv[:, cc, :], start=(cc == 0), stop=(cc == CC - 1), wait=w)
        sq_evt = {}
        sqmm_evt = {}
        for cc in range(2):
            S.op("vector", "tensor_tensor", out=sq2[:, cc, :], in0=dv[:, cc, :],
                 in1=dv[:, cc, :], op=ALU.mult,
                 wait=[("dve", cast_evt[cc])], inc=[("dve", 1)])
            sq_evt[cc] = S.now("dve")
        for cc in range(CC):
            S.op("tensor", "matmul", ps[:, TTS:2 * TTS], lhsT=onesm.ap(),
                 rhs=sq2[:, cc % 2, :], start=(cc == 0), stop=(cc == CC - 1),
                 wait=[("dve", sq_evt[cc])], inc=[("pe", 1)])
            sqmm_evt[cc] = S.now("pe")
            if cc + 2 < CC:
                S.op("vector", "tensor_tensor", out=sq2[:, cc % 2, :],
                     in0=dv[:, cc + 2, :], in1=dv[:, cc + 2, :], op=ALU.mult,
                     wait=[("pe", sqmm_evt[cc]), ("dve", cast_evt[cc + 2])],
                     inc=[("dve", 1)])
                sq_evt[cc + 2] = S.now("dve")
        var_dst = vstage.ap()
        stats_pe = S.now("pe")
        S.op("scalar", "mul", sq2[:, 0, :], ps[:, 0:TTS], 1.0 / D,
             wait=[("pe", stats_pe)], inc=[("act", 1)])
        S.wait("vector", "act", S.now("act"))
        S.op("vector", "tensor_tensor", out=var_dst, in0=sq2[:, 0, :],
             in1=sq2[:, 0, :], op=ALU.mult)
        S.op("vector", "scalar_tensor_tensor", out=var_dst,
             in0=ps[:, TTS:2 * TTS], scalar=1.0 / D, in1=var_dst,
             op0=ALU.mult, op1=ALU.subtract, inc=[("dve", 1)])
        S.wait("scalar", "dve", S.now("dve"))
        S.op("scalar", "activation", out=ps[:, 0:TTS], in_=var_dst,
             func=AF.Ln, bias=epsv.ap(), scale=1.0)
        S.op("scalar", "activation", out=(sq2[:, 1, :] if r_in_sq2 else scr.ap()),
             in_=ps[:, 0:TTS], func=AF.Exp, scale=-0.5, inc=[("act", 1)])
        return S.now("act")

    def ln_apply(tt, r_act, dst=None, r_src=None):
        dv = xl if dst is None else dst
        rv = scr.ap() if r_src is None else r_src
        tsl = slice(tt * TTS, (tt + 1) * TTS)
        S.wait("vector", "act", r_act)
        for cc in range(CC):
            S.op("vector", "tensor_tensor", out=dv[:, cc, :],
                 in0=xr[:, cc, tsl], in1=sq2[:, 0, :], op=ALU.subtract)
            S.op("vector", "tensor_tensor", out=dv[:, cc, :], in0=dv[:, cc, :],
                 in1=rv, op=ALU.mult,
                 inc=[("dve", 1)] if cc == CC - 1 else [])

    def layernorm(tt):
        S.barrier()
        r_act = ln_stats(tt, paa)
        ln_apply(tt, r_act)

    def ffn(tt, ln_next=None, xv=None):
        S.barrier()
        if xv is None:
            xv = xl
        tsl = slice(tt * TTS, (tt + 1) * TTS)
        xln_ready = S.now("dve")
        ev_mm = {}
        ev_gevac = {}
        ev_fm = {}
        wd_g = {}
        wd_u = {}

        def issue_w(ic):
            pr = ic % 3
            bwait("gpsimd", bufstate["ws"][2 * pr])
            bwait("gpsimd", bufstate["ws"][2 * pr + 1])
            S.op("gpsimd", "dma_start", out=ws[:, 2 * pr, :], in_=wg_l[ic],
                 inc=[("w%d" % (2 * pr), 16)])
            wd_g[ic] = ("w%d" % (2 * pr), S.now("w%d" % (2 * pr)))
            S.op("gpsimd", "dma_start", out=ws[:, 2 * pr + 1, :], in_=wu_l[ic],
                 inc=[("w%d" % (2 * pr + 1), 16)])
            wd_u[ic] = ("w%d" % (2 * pr + 1), S.now("w%d" % (2 * pr + 1)))

        issue_w(0)
        issue_w(1)
        issue_w(2)
        for ic in range(NIC):
            bsel = ic % 2
            pr = ic % 3
            for cc in range(CC):
                w = []
                if cc == 0:
                    w = [wd_g[ic], ("dve", xln_ready)]
                    if ic >= 2:
                        w.append(("act", ev_gevac[ic - 2]))
                S.op("tensor", "matmul", paa[:, bsel * TTS:(bsel + 1) * TTS],
                     lhsT=ws[:, 2 * pr, cc * 128:(cc + 1) * 128],
                     rhs=xv[:, cc, :], start=(cc == 0), stop=(cc == CC - 1),
                     wait=w, inc=[("pe", 1)] if cc == CC - 1 else [])
            gate_pe = S.now("pe")
            for cc in range(CC):
                S.op("tensor", "matmul", pba[:, bsel * TTS:(bsel + 1) * TTS],
                     lhsT=ws[:, 2 * pr + 1, cc * 128:(cc + 1) * 128],
                     rhs=xv[:, cc, :], start=(cc == 0), stop=(cc == CC - 1),
                     wait=[wd_u[ic]] if cc == 0 else (),
                     inc=[("pe", 1)] if cc == CC - 1 else [])
            ev_mm[ic] = S.now("pe")
            bufstate["ws"][2 * pr] = bufstate["ws"][2 * pr + 1] = ("pe", ev_mm[ic])
            if ic + 3 < NIC:
                issue_w(ic + 3)
            g_sb = gub.ap()[:, bsel * TTS:(bsel + 1) * TTS]
            u_sb = gub.ap()[:, (2 + bsel) * TTS:(3 + bsel) * TTS]
            w = [("pe", gate_pe)]
            if ic >= 2:
                w.append(("dve", ev_fm[ic - 2]))
            S.op("scalar", "activation", out=g_sb,
                 in_=paa[:, bsel * TTS:(bsel + 1) * TTS], func=AF.Silu,
                 scale=sgs.ap()[:, ic:ic + 1], wait=w, inc=[("act", 1)])
            S.op("scalar", "activation", out=u_sb,
                 in_=pba[:, bsel * TTS:(bsel + 1) * TTS], func=AF.Copy,
                 scale=sus.ap()[:, ic:ic + 1], wait=[("pe", ev_mm[ic])],
                 inc=[("act", 1)])
            ev_gevac[ic] = S.now("act")
            S.op("vector", "tensor_tensor", out=fu[:, ic, :], in0=g_sb,
                 in1=u_sb, op=ALU.mult, wait=[("act", ev_gevac[ic])],
                 inc=[("dve", 1)])
            ev_fm[ic] = S.now("dve")
        fuse_done = S.now("dve")
        ev_dmm = {}
        ev_bank = {}
        ev_add = {}
        for q in range(4):
            for io in range(8):
                bsel = io % 2
                bwait("gpsimd", bufstate["wdb"][bsel])
                S.op("gpsimd", "dma_start", out=wdbv[:, bsel, :, :],
                     in_=wd_l[q, io].rearrange("p (k o) -> p k o", k=8),
                     inc=[("wd%d" % bsel, 16)])
                wready = ("wd%d" % bsel, S.now("wd%d" % bsel))
                for k in range(8):
                    for o in range(4):
                        w = []
                        if k == 0 and o == 0:
                            w = [wready]
                            if io == 0:
                                w.append(("dve", fuse_done))
                        if k == 0 and io == 0 and q > 0:
                            w.append(("act", ev_bank[(q - 1, o)]))
                        S.op("tensor", "matmul", pdna[:, o * TTS:(o + 1) * TTS],
                             lhsT=wdbv[:, bsel, k, o * 128:(o + 1) * 128],
                             rhs=fu[:, io * 8 + k, :],
                             start=(io == 0 and k == 0), stop=(io == 7 and k == 7),
                             wait=w,
                             inc=[("pe", 1)] if (k == 7 and o == 3) else [])
                ev_dmm[(q, io)] = S.now("pe")
                bufstate["wdb"][bsel] = ("pe", ev_dmm[(q, io)])
                if q == 0 and io == 0 and ln_next is not None:
                    r_act = ln_stats(ln_next, paa,
                                     cast_wait=[("pe", ev_mm[NIC - 1])],
                                     stats_wait=[("act", ev_gevac[NIC - 1])])
                    ln_apply(ln_next, r_act)
            qmm_pe = S.now("pe")
            for o in range(4):
                oc = q * 4 + o
                gi = q * 4 + o
                buf = dtmp if gi % 2 == 0 else rtmp
                w = [("pe", qmm_pe)]
                if (q, o - 2) in ev_add:
                    w.append(("dve", ev_add[(q, o - 2)]))
                elif q > 0 and o < 2:
                    w.append(("dve", ev_add[(q - 1, o + 2)]))
                S.op("scalar", "activation", out=buf.ap(),
                     in_=pdna[:, o * TTS:(o + 1) * TTS], func=AF.Copy,
                     scale=sdh.ap()[:, oc:oc + 1], wait=w, inc=[("act", 1)])
                ev_bank[(q, o)] = S.now("act")
                S.op("vector", "tensor_tensor", out=xr[:, oc, tsl],
                     in0=xr[:, oc, tsl], in1=buf.ap(), op=ALU.add,
                     wait=[("act", ev_bank[(q, o)])], inc=[("dve", 1)])
                ev_add[(q, o)] = S.now("dve")

    qkv_state = {"i": 0, "vtr_dve": 0, "ev_evac": [0, 0], "xln_ready": 0}

    def qkv(tt, kinds, xv=None, bar=True):
        if bar:
            S.barrier()
        if xv is None:
            xv = xl
        xln_ready = S.now("dve")
        ev_slot = {}
        ev_evac = {}
        state = qkv_state
        state["xln_ready"] = xln_ready

        def proj(kind, n_oc, w_dram, bias_sb):
            xln_ready = state["xln_ready"]
            for oc in range(n_oc):
                slot = state["i"] % 5
                par = state["i"] % 2
                bwait("gpsimd", bufstate["ws"][slot])
                S.op("gpsimd", "dma_start", out=ws[:, slot, :], in_=w_dram[oc],
                     inc=[("w%d" % slot, 16)])
                wready = ("w%d" % slot, S.now("w%d" % slot))
                for cc in range(CC):
                    w = []
                    if cc == 0:
                        w = [wready, ("dve", xln_ready)]
                        if state["ev_evac"][par]:
                            w.append(("act", state["ev_evac"][par]))
                    S.op("tensor", "matmul", paa[:, par * TTS:(par + 1) * TTS],
                         lhsT=ws[:, slot, cc * 128:(cc + 1) * 128],
                         rhs=xv[:, cc, :], start=(cc == 0), stop=(cc == CC - 1),
                         wait=w, inc=[("pe", 1)] if cc == CC - 1 else [])
                ev_slot[slot] = S.now("pe")
                bufstate["ws"][slot] = ("pe", ev_slot[slot])
                psrc = paa[:, par * TTS:(par + 1) * TTS]
                if kind == "v":
                    S.op("scalar", "activation", out=vstage.ap(), in_=psrc,
                         func=AF.Identity, bias=bias_sb[:, oc:oc + 1], scale=1.0,
                         wait=[("pe", ev_slot[slot]), ("dve", state["vtr_dve"])],
                         inc=[("act", 1)])
                    state["ev_evac"][par] = S.now("act")
                    for j in range(4):
                        w = []
                        if j == 0:
                            w = [("act", state["ev_evac"][par]), ("dve", state["vtr_dve"])]
                        S.op("tensor", "transpose",
                             pdnb[:, j * 128:(j + 1) * 128],
                             vstage.ap()[:, j * 128:(j + 1) * 128], idb.ap(),
                             wait=w, inc=[("pe", 1)] if j == 3 else [])
                    S.op("vector", "tensor_copy",
                         out=vov[:, tt * 4:(tt + 1) * 4, oc * 128:(oc + 1) * 128],
                         in_=pdnb[:, 0:TTS].rearrange("p (j t) -> p j t", j=4),
                         wait=[("pe", S.now("pe"))], inc=[("dve", 1)])
                    state["vtr_dve"] = S.now("dve")
                else:
                    S.op("scalar", "activation", out=rtmp.ap(), in_=psrc,
                         func=AF.Identity, bias=bias_sb[:, oc:oc + 1], scale=1.0,
                         wait=[("pe", ev_slot[slot]), ("dve", S.now("dve"))],
                         inc=[("act", 1)])
                    state["ev_evac"][par] = S.now("act")
                    cosv = (cosq if kind == "q" else cosk).ap()[:, oc:oc + 1]
                    sinv = (sinq if kind == "q" else sink).ap()[:, oc:oc + 1]
                    dest = (qpv[:, oc, tt * TTS:(tt + 1) * TTS] if kind == "q"
                            else kov[:, oc, tt * TTS:(tt + 1) * TTS])
                    S.wait("vector", "act", state["ev_evac"][par])
                    S.op("vector", "tensor_copy", out=scr.ap()[0:64, :],
                         in_=rtmp.ap()[64:128, :])
                    S.op("vector", "tensor_copy", out=scr.ap()[64:128, :],
                         in_=rtmp.ap()[0:64, :])
                    S.op("vector", "tensor_scalar", out=scr.ap(), in0=scr.ap(),
                         scalar1=sinv, scalar2=None, op0=ALU.mult)
                    S.op("vector", "scalar_tensor_tensor", out=dest,
                         in0=rtmp.ap(), scalar=cosv, in1=scr.ap(),
                         op0=ALU.mult, op1=ALU.add, inc=[("dve", 1)])
                state["i"] += 1

        for kind in kinds:
            if kind == "k":
                proj("k", KV, wk_l, bks.ap())
            elif kind == "v":
                proj("v", KV, wv_l, bvs.ap())
            else:
                proj("q", H, wq_l, bqs.ap())

    def exchange_send(tt):
        dve_done = S.now("dve")
        S.wait("sync", "dve", dve_done)
        ksem = "kv%d" % tt
        S.op("sync", "dma_start",
             out=cc_in[tt].ap()[0:KV * 128, :].rearrange("(kv p) t -> p kv t", p=128),
             in_=kov[:, :, tt * TTS:(tt + 1) * TTS], inc=[(ksem, 16)])
        S.op("sync", "dma_start",
             out=cc_in[tt].ap()[KV * 128:2 * KV * 128, :].rearrange("(s p) d -> p s d", p=128),
             in_=vov[:, tt * 4:(tt + 1) * 4, :], inc=[(ksem, 16)])
        S.wait("gpsimd", ksem, S.now(ksem))
        S.op("gpsimd", "collective_compute", "AllGather", ALU.bypass,
             replica_groups=[[0, 1], [2, 3], [4, 5], [6, 7]],
             ins=[cc_in[tt].ap().opt()], outs=[cc_out[tt].ap().opt()],
             inc=[("cc", 1)])

    def exchange_recv():
        ios = []
        for tt in range(2):
            q = "sync" if tt == 0 else "gpsimd"
            rsem = "rc%d" % tt
            S.wait(q, "cc", tt + 1)
            for r in range(2):
                base = r * 2 * KV * 128
                S.op(q, "dma_start",
                     out=kfv[:, :, r * TPC + tt * TTS: r * TPC + (tt + 1) * TTS],
                     in_=cc_out[tt].ap()[base:base + KV * 128, :]
                         .rearrange("(kv p) t -> p kv t", p=128),
                     inc=[(rsem, 16)])
                S.op(q, "dma_start",
                     out=vtv[:, r * 8 + tt * 4: r * 8 + (tt + 1) * 4, :],
                     in_=cc_out[tt].ap()[base + KV * 128:base + 2 * KV * 128, :]
                         .rearrange("(s p) d -> p s d", p=128),
                     inc=[(rsem, 16)])
            ios.append((rsem, S.now(rsem)))
        return ios

    def load_mask(tt, free_pe):
        S.wait("sync", "pe", free_pe)
        S.op("sync", "dma_start",
             out=masksb,
             in_=mask_d[:, tt * TTS:(tt + 1) * TTS].rearrange("(s p) t -> p s t", p=128),
             inc=[("msk", 16)])
        return S.now("msk")

    SIG = (0, 1, 2, 3, 8, 9, 10, 11, 4, 5, 6, 7, 12, 13, 14, 15)

    def scores(tt, kv_io0, kv_io1, mask_io):
        S.barrier()
        pslot = [paa[:, 0:TTS], paa[:, TTS:2 * TTS],
                 pdna[:, 2 * TTS:3 * TTS], pdna[:, 3 * TTS:4 * TTS]]
        ppair = [paa[:, 0:2 * TTS], pdna[:, 2 * TTS:4 * TTS]]
        exp_evt = {}
        avz_evt = {}
        out_dve = [0, 0]
        z_act = [0, 0]

        def emit_avz(h, j, first_banks_w=()):
            kv = h // GRP
            s = SIG[j]
            pz = pba if h % 2 == 0 else pdna
            w = [("act", exp_evt[(h, j)])]
            if j == 0:
                w += list(first_banks_w)
            S.op("tensor", "matmul", pz[:, 0:TTS],
                 lhsT=vtv[:, s, kv * 128:(kv + 1) * 128], rhs=eTv[:, j % 4, :],
                 start=(j == 0), stop=(j == 15), wait=w)
            S.op("tensor", "matmul", pz[:, TTS:2 * TTS],
                 lhsT=onesm.ap(), rhs=eTv[:, j % 4, :],
                 start=(j == 0), stop=(j == 15), inc=[("pe", 1)])
            avz_evt[(h, j)] = S.now("pe")

        def emit_norm(h):
            pz = pba if h % 2 == 0 else pdna
            zs = dtmp if h % 2 == 0 else rtmp
            S.op("scalar", "activation", out=scr.ap(), in_=pz[:, TTS:2 * TTS],
                 func=AF.Ln, wait=[("pe", avz_evt[(h, 15)])])
            S.op("scalar", "activation", out=zs.ap(), in_=scr.ap(), func=AF.Exp,
                 scale=-1.0, inc=[("act", 1)])
            z_act[h % 2] = S.now("act")
            S.op("vector", "tensor_tensor",
                 out=ofv[:, h, :],
                 in0=pz[:, 0:TTS], in1=zs.ap(), op=ALU.mult,
                 wait=[("act", S.now("act"))], inc=[("dve", 1)])
            out_dve[h % 2] = S.now("dve")

        for h in range(H):
            kv = h // GRP
            qtile = qpv[:, h, tt * TTS:(tt + 1) * TTS]
            mmset = set(range(4 * tt, 4 * tt + 4)) | set(range(8 + 4 * tt, 12 + 4 * tt))
            for j in range(16):
                s = SIG[j]
                w = []
                if h == 0 and j == 0:
                    w = [kv_io0, ("msk", mask_io)]
                if h == 0 and j == 8:
                    w = [kv_io1]
                key = (h, j - 4) if j >= 4 else ((h - 1, 12 + j) if h > 0 else None)
                if key is not None:
                    w.append(("act", exp_evt[key]))
                has_mm = s in mmset
                S.op("tensor", "matmul", pslot[j % 4],
                     lhsT=kfv[:, kv, s * 128:(s + 1) * 128], rhs=qtile,
                     start=True, stop=not has_mm, wait=w,
                     inc=[] if has_mm else [("pe", 1)])
                if has_mm:
                    S.op("tensor", "matmul", pslot[j % 4],
                         lhsT=idb.ap(), rhs=masksb[:, s, :],
                         start=False, stop=True, inc=[("pe", 1)])
                score_pe = S.now("pe")
                key = (h, j - 4) if j >= 4 else ((h - 1, 12 + j) if h > 0 else None)
                w = [("pe", score_pe)]
                if key is not None:
                    w.append(("pe", avz_evt[key]))
                S.op("scalar", "activation", out=eTv[:, j % 4, :],
                     in_=pslot[j % 4], func=AF.Exp, scale=1.0,
                     bias=mbias.ap()[:, tt * 16 + s: tt * 16 + s + 1],
                     wait=w, inc=[("act", 1)])
                exp_evt[(h, j)] = S.now("act")
                if j >= 3:
                    fb = ()
                    if j == 3:
                        fb = [("dve", out_dve[h % 2]), ("act", z_act[h % 2])]
                    emit_avz(h, j - 3, fb)
            emit_avz(h, 13)
            emit_avz(h, 14)
            emit_avz(h, 15)
            emit_norm(h)
        bufstate["ws"][4] = ("pe", S.now("pe"))
        return S.now("pe")

    def oproj(tt, ln_next=None):
        S.barrier()
        of_ready = S.now("dve")
        ev_slot = {}
        ev_evac = {}
        for oc in range(16):
            if oc == 8 and ln_next is not None:
                r_act = ln_stats(ln_next, pba)
                ln_apply(ln_next, r_act)
            slot = oc % 4
            par = oc % 2
            bwait("gpsimd", bufstate["ws"][slot])
            S.op("gpsimd", "dma_start", out=ws[:, slot, :], in_=wo_l[oc],
                 inc=[("w%d" % slot, 16)])
            wready = ("w%d" % slot, S.now("w%d" % slot))
            for cc in range(CC):
                w = []
                if cc == 0:
                    w = [wready, ("dve", of_ready)]
                    if oc >= 2:
                        w.append(("act", ev_evac[par]))
                S.op("tensor", "matmul", paa[:, par * TTS:(par + 1) * TTS],
                     lhsT=ws[:, slot, cc * 128:(cc + 1) * 128],
                     rhs=ofv[:, cc, :],
                     start=(cc == 0), stop=(cc == CC - 1), wait=w,
                     inc=[("pe", 1)] if cc == CC - 1 else [])
            ev_slot[slot] = S.now("pe")
            bufstate["ws"][slot] = ("pe", ev_slot[slot])
            if oc == 15:
                bufstate["ws"][5] = ("pe", ev_slot[slot])
                bufstate["wdb"][0] = ("pe", ev_slot[slot])
                bufstate["wdb"][1] = ("pe", ev_slot[slot])
            S.op("scalar", "activation", out=dtmp.ap(),
                 in_=paa[:, par * TTS:(par + 1) * TTS], func=AF.Identity,
                 bias=bos.ap()[:, oc:oc + 1], scale=1.0,
                 wait=[("pe", ev_slot[slot]), ("dve", S.now("dve"))],
                 inc=[("act", 1)])
            ev_evac[par] = S.now("act")
            S.op("vector", "tensor_tensor",
                 out=xr[:, oc, tt * TTS:(tt + 1) * TTS],
                 in0=xr[:, oc, tt * TTS:(tt + 1) * TTS], in1=dtmp.ap(),
                 op=ALU.add, wait=[("act", ev_evac[par])], inc=[("dve", 1)])

    have1, haveA, have2 = ("ffn1" in parts), ("attn" in parts), ("ffn2" in parts)
    EMB = EMBED_LN
    if parts:
        r0 = ln_stats(0, pba, cast_wait=[("act", input_evts[3][0])],
                      stats_wait=[("pe", input_evts[3][1])])
        ln_apply(0, r0)
    if have1:
        ffn(0, ln_next=1 if EMB else None)
        if not EMB:
            layernorm(1)
        ffn(1, ln_next=(0 if (haveA or have2) else None) if EMB else None)
    if haveA:
        if have1 and not EMB:
            layernorm(0)
        qkv(0, ("k", "v"))
        exchange_send(0)
        r21 = ln_stats(1, pba, dst=xl2, r_in_sq2=True)
        ln_apply(1, r21, dst=xl2, r_src=sq2[:, 1, :])
        qkv(0, ("q",), bar=False)
        q0_pe = S.now("pe")
        qkv(1, ("k", "v"), xv=xl2, bar=False)
        exchange_send(1)
        qkv(1, ("q",), xv=xl2, bar=False)
        mask_io0 = load_mask(0, q0_pe)
        kv_io0, kv_io1 = exchange_recv()
        mfree = scores(0, kv_io0, kv_io1, mask_io0)
        oproj(0)
        mask_io1 = load_mask(1, mfree)
        scores(1, kv_io0, kv_io1, mask_io1)
        oproj(1, ln_next=0 if (have2 and EMB) else None)
    if have2:
        if (have1 or haveA) and not EMB:
            layernorm(0)
        ffn(0, ln_next=1 if EMB else None)
        if not EMB:
            layernorm(1)
        ffn(1, ln_next=None)

    S.barrier()
    ev_grp = {}
    ev_dma = {}
    g = 0
    first = True
    for tb in range(TPC // 128):
        buf = tb % 2
        for ccg in range(CC // 4):
            par = g % 2
            w = []
            if first:
                w = [("dve", S.now("dve")), ("act", S.now("act"))]
                first = False
            if g >= 2:
                w.append(("act", ev_grp[g - 2]))
            for j in range(4):
                cc = ccg * 4 + j
                S.op("tensor", "transpose",
                     paa[:, par * TTS + j * 128: par * TTS + (j + 1) * 128],
                     xr[:, cc, tb * 128:(tb + 1) * 128], idf.ap(),
                     wait=w if j == 0 else (),
                     inc=[("pe", 1)] if j == 3 else [])
            wloc = [("pe", S.now("pe"))]
            if ccg == 0 and tb >= 2:
                wloc.append(("ip%d" % buf, ev_dma[tb - 2]))
            S.op("scalar", "copy",
                 out=tokv[:, buf * D + ccg * 512: buf * D + (ccg + 1) * 512],
                 in_=paa[:, par * TTS:(par + 1) * TTS],
                 wait=wloc, inc=[("act", 1)])
            ev_grp[g] = S.now("act")
            g += 1
        S.op("sync", "dma_start", out=y_out[tb * 128:(tb + 1) * 128, :],
             in_=tokv[:, buf * D:(buf + 1) * D],
             wait=[("act", S.now("act"))], inc=[("ip%d" % buf, 16)])
        ev_dma[tb] = S.now("ip%d" % buf)
    S.wait("sync", "ip0", S.now("ip0"))
    S.wait("sync", "ip1", S.now("ip1"))

    with nc.Block() as block:
        @block.tensor
        def _(e):
            for fn in S.q["tensor"]:
                fn(e, sems)

        @block.scalar
        def _(e):
            for fn in S.q["scalar"]:
                fn(e, sems)

        @block.vector
        def _(e):
            for fn in S.q["vector"]:
                fn(e, sems)

        @block.sync
        def _(e):
            for fn in S.q["sync"]:
                fn(e, sems)

        @block.gpsimd
        def _(e):
            for fn in S.q["gpsimd"]:
                fn(e, sems)

    return nc


def _host_prep(inputs):
    gi = lambda k: np.ascontiguousarray(np.asarray(inputs[k]))
    x = gi("L0_attn_x")
    wq = gi("L0_attn_wq"); bq = gi("L0_attn_bq")
    wk = gi("L0_attn_wk"); bk = gi("L0_attn_bk")
    wv = gi("L0_attn_wv"); bv = gi("L0_attn_bv")
    wo = gi("L0_attn_wo"); bo = gi("L0_attn_bo")
    wg = gi("L0_ffn_w_gate"); wu = gi("L0_ffn_w_up"); wd = gi("L0_ffn_w_down")
    sgv = gi("L0_ffn_s_gate"); suv = gi("L0_ffn_s_up"); sdv = gi("L0_ffn_s_down")

    def tile_oc(w, noc):
        return np.ascontiguousarray(
            w.reshape(noc, 128, CC, 128).transpose(0, 3, 2, 1).reshape(noc, 128, D))

    wg_l = np.ascontiguousarray(
        wg.reshape(CC, 128, NIC, 128).transpose(2, 1, 0, 3).reshape(NIC, 128, D))
    wu_l = np.ascontiguousarray(
        wu.reshape(CC, 128, NIC, 128).transpose(2, 1, 0, 3).reshape(NIC, 128, D))
    wd_l = np.ascontiguousarray(
        wd.reshape(8, 8, 128, 4, 512).transpose(3, 0, 2, 1, 4).reshape(4, 8, 128, 8 * 512))

    pp = lambda v, n: np.ascontiguousarray(v.reshape(n, 128).T.astype(np.float32))
    consts = {
        "wq": tile_oc(wq, 16), "wk": tile_oc(wk, 4), "wv": tile_oc(wv, 4),
        "wo": tile_oc(wo, 16), "wg": wg_l, "wu": wu_l, "wdn": wd_l,
        "sg": pp(sgv, NIC), "su": pp(suv, NIC), "sd": pp(sdv, CC),
        "bq": pp(bq, 16), "bk": pp(bk, 4), "bv": pp(bv, 4), "bo": pp(bo, 16),
    }
    j = np.arange(64, dtype=np.float64)
    inv = 1.0 / (10000.0 ** (2.0 * j / 128.0))
    scl = 1.0 / np.sqrt(128.0)
    for name, nh, s in (("q", 16, scl), ("k", 4, 1.0)):
        hh = np.arange(nh, dtype=np.float64)
        ang = inv[:, None] * hh[None, :]
        consts["cos" + name] = np.ascontiguousarray(
            (np.concatenate([np.cos(ang), np.cos(ang)], 0) * s).astype(np.float32))
        consts["sin" + name] = np.ascontiguousarray(
            (np.concatenate([-np.sin(ang), np.sin(ang)], 0) * s).astype(np.float32))
    consts["idb"] = np.eye(128, dtype=ml_dtypes.bfloat16)
    consts["idf"] = np.eye(128, dtype=np.float32)
    consts["onesm"] = np.ones((128, 128), dtype=ml_dtypes.bfloat16)
    consts["epsv"] = np.full((128, 1), EPS, dtype=np.float32)

    maskT = np.tril(np.ones((T, T), dtype=np.float32)).T.astype(ml_dtypes.bfloat16)
    in_maps = []
    for c in range(8):
        b, half = c // 2, c % 2
        m = dict(consts)
        m["x"] = np.ascontiguousarray(x[b, half * TPC:(half + 1) * TPC, :])
        m["mask"] = np.ascontiguousarray(maskT[:, half * TPC:(half + 1) * TPC])
        mb = np.zeros((128, 32), np.float32)
        for tt in range(2):
            t0 = half * TPC + tt * TTS
            mmset = set(range(4 * tt, 4 * tt + 4)) | set(range(8 + 4 * tt, 12 + 4 * tt))
            for s in range(16):
                if s not in mmset and s * 128 + 127 <= t0:
                    mb[:, tt * 16 + s] = 1.0
        m["mbias"] = mb
        in_maps.append(m)
    return in_maps


def kernel(**inputs):
    nc = _build(ALL_PARTS)
    in_maps = _host_prep(inputs)
    res = run_bass_kernel_spmd(nc, in_maps, core_ids=list(range(8)))
    out = np.empty((B, T, D), np.float32)
    for c in range(8):
        b, half = c // 2, c % 2
        out[b, half * TPC:(half + 1) * TPC, :] = res.results[c]["y"]
    return out
